# version 08
# baseline (speedup 1.0000x reference)
import math
import numpy as np
import ml_dtypes
import concourse.bass as bass
import concourse.mybir as mybir
from concourse.bass_utils import run_bass_kernel_spmd
from concourse.alu_op_type import AluOpType as ALU

F32 = mybir.dt.float32
BF16 = mybir.dt.bfloat16
AF = mybir.ActivationFunctionType
AX = mybir.AxisListType

NCORES = 8
NB = 2
N = 2048
HID = 2048
NT = N // 128
DC = HID // 128
H, DH, HKV, GRP = 32, 64, 8, 4
NCH = N // 128
FF = 4096
FC = FF // 128
EPASS = 4
ETOK = (NB * N) // EPASS
ET8 = ETOK // 128
C_SOFT = 12.0
B_LEVEL = 4
SCALE = 1.0 / math.sqrt(DH)


class Op:
    __slots__ = ("eng", "fn", "idx", "inc_no", "waits", "drain", "dma", "tag")

    def __init__(self, eng, fn, idx, dma, tag):
        self.eng, self.fn, self.idx, self.dma, self.tag = eng, fn, idx, dma, tag
        self.inc_no = None
        self.waits = []
        self.drain = False


class Track:
    ENGS = ("t", "v", "s", "g", "y")
    PSUMTOK = {"PQ0", "PQ1", "PQ2", "PQ3", "B4", "B5", "B6", "B7"}

    def __init__(self):
        self.ops = {e: [] for e in self.ENGS}
        self.last_writer = {}
        self.readers = {}
        self.drain_cover = {e: -1 for e in self.ENGS}
        self.streams = {}
        self.last_dma = {}

    @staticmethod
    def _tok(t):
        if isinstance(t, (str, tuple, int)):
            return t
        return ("H", t.name)

    def op(self, eng, fn, reads=(), writes=(), dma=None, tag=""):
        reads = [self._tok(t) for t in reads]
        writes = [self._tok(t) for t in writes]
        o = Op(eng, fn, len(self.ops[eng]), dma, tag)
        if dma is not None:
            k = self.streams.get(dma, 0) + 1
            self.streams[dma] = k
            o.inc_no = 16 * k
            self.last_dma[dma] = o
        deps = {}

        def add(d):
            if d is o:
                return
            key = ("d", d.dma) if d.dma is not None else ("e", d.eng)
            cur = deps.get(key)
            if cur is None or d.idx > cur.idx:
                deps[key] = d

        for t in reads:
            w = self.last_writer.get(t)
            if w is not None:
                add(w)
            if t in self.PSUMTOK and eng in ("s", "v"):
                for rd in self.readers.get(t, ()):
                    if rd.eng in ("s", "v") and rd.eng != eng:
                        add(rd)
        for t in writes:
            w = self.last_writer.get(t)
            if w is not None:
                add(w)
            for rd in self.readers.get(t, ()):
                add(rd)
        need_drain = False
        for d in deps.values():
            if d.dma is not None:
                o.waits.append(d)
            elif d.eng == eng:
                if eng != "t" and d.idx > self.drain_cover[eng]:
                    need_drain = True
            else:
                o.waits.append(d)
        if need_drain:
            o.drain = True
            self.drain_cover[eng] = o.idx - 1
        self.ops[eng].append(o)
        for t in reads:
            self.readers.setdefault(t, []).append(o)
        for t in writes:
            self.last_writer[t] = o
            self.readers[t] = []
        return o

    def barrier(self):
        dmas = list(self.last_dma.values())
        for e in self.ENGS:
            o = self.op(e, (lambda eng: eng.drain()),
                        reads=[("BAR", e2) for e2 in self.ENGS if e2 != e],
                        writes=[("BAR", e)], tag=f"bar_{e}")
            for d in dmas:
                if d not in o.waits:
                    o.waits.append(d)

    def finalize(self):
        for e in self.ENGS:
            for o in self.ops[e]:
                for d in o.waits:
                    if d.dma is None and d.inc_no is None:
                        d.inc_no = -1
        self.inc_total = {}
        for e in self.ENGS:
            c = 0
            for o in self.ops[e]:
                if o.dma is None and o.inc_no is not None:
                    c += 1
                    o.inc_no = c
            self.inc_total[e] = c

    def emit_engine(self, e, eng, sems, dsems):
        waited = {}
        for o in self.ops[e]:
            if o.drain:
                eng.drain()
            for d in o.waits:
                if d.dma is not None:
                    key, sem, val = ("d", d.dma), dsems[d.dma], d.inc_no
                else:
                    key, sem, val = ("e", d.eng), sems[d.eng], d.inc_no
                assert val is not None and val > 0, (o.tag, d.tag)
                if waited.get(key, 0) < val:
                    eng.wait_ge(sem, val)
                    waited[key] = val
            if o.dma is not None and o.inc_no > 16:
                key = ("d", o.dma)
                if waited.get(key, 0) < o.inc_no - 16:
                    eng.wait_ge(dsems[o.dma], o.inc_no - 16)
                    waited[key] = o.inc_no - 16
            inst = o.fn(eng)
            if o.dma is not None:
                inst.then_inc(dsems[o.dma], 16)
            elif o.inc_no is not None:
                inst.then_inc(sems[e], 1)


def build_program(phases="ABQCDE", nb=NB, ngroups=HKV, npass=EPASS, dbg=False):
    nc = bass.Bass("TRN2", target_bir_lowering=False)
    tr = Track()

    x_in = nc.dram_tensor("x", [NB * N, HID], F32, kind="ExternalInput").ap()
    Wq_d = nc.dram_tensor("Wq", [HID, H * DH], F32, kind="ExternalInput").ap()
    Wk_d = nc.dram_tensor("Wk", [HID, HKV * DH], F32, kind="ExternalInput").ap()
    Wv_d = nc.dram_tensor("Wv", [HID, HKV * DH], F32, kind="ExternalInput").ap()
    Wo_d = nc.dram_tensor("Wo", [H * DH, HID], F32, kind="ExternalInput").ap()
    W1_d = nc.dram_tensor("W1T", [HID, FF], F32, kind="ExternalInput").ap()
    W2_d = nc.dram_tensor("W2T", [FF, FF], F32, kind="ExternalInput").ap()
    W3_d = nc.dram_tensor("W3T", [FF, HID], F32, kind="ExternalInput").ap()
    b1_d = nc.dram_tensor("b1", [128, FF // 128], F32, kind="ExternalInput").ap()
    b2_d = nc.dram_tensor("b2", [128, FF // 128], F32, kind="ExternalInput").ap()
    b3_d = nc.dram_tensor("b3", [HID], F32, kind="ExternalInput").ap()
    i128_d = nc.dram_tensor("i128", [128, 128], BF16, kind="ExternalInput").ap()
    nstr_d = nc.dram_tensor("negstrict", [128, 256], BF16, kind="ExternalInput").ap()
    mtri_d = nc.dram_tensor("masktriu", [128, 128], BF16, kind="ExternalInput").ap()
    mneg_d = nc.dram_tensor("maskneg", [128, 128], BF16, kind="ExternalInput").ap()
    segs_d = nc.dram_tensor("segones", [128, 2], BF16, kind="ExternalInput").ap()
    cvec_d = nc.dram_tensor("constvec", [128, 16], F32, kind="ExternalInput").ap()
    out_d = nc.dram_tensor("out", [NB * N, HID], F32, kind="ExternalOutput").ap()

    KIND = "ExternalOutput" if dbg else "Internal"
    kT_s = nc.dram_tensor("kT_s", [HKV, 128, N], BF16, kind=KIND).ap()
    klT_s = nc.dram_tensor("klT_s", [HKV, 128, N], BF16, kind=KIND).ap()
    kbT_s = nc.dram_tensor("kbT_s", [HKV, 64, N], BF16, kind=KIND).ap()
    klt_s = nc.dram_tensor("klt_s", [HKV, N, DH], BF16, kind=KIND).ap()
    vbkb_s = nc.dram_tensor("vbkb_s", [HKV, N, 128], BF16, kind=KIND).ap()
    vaug_s = nc.dram_tensor("vaug_s", [HKV, N, 66], BF16, kind=KIND).ap()
    ocomb_s = nc.dram_tensor("ocomb_s", [HKV, N, 256], BF16, kind=KIND).ap()
    x1_s = nc.dram_tensor("x1_s", [NB * N, HID], F32, kind=KIND).ap()
    o3_s = nc.dram_tensor("o3_s", [ETOK, HID], F32, kind=KIND).ap()
    dbgout = nc.dram_tensor("dbgout", [16 * N, HID], BF16, kind=KIND).ap() if dbg else None

    BASE = (nc.sbuf_base + 255) // 256 * 256
    LIMIT = nc.sbuf_top - BASE

    def at(name, shape, dtype, off):
        sz = shape[1] * {F32: 4, BF16: 2}[dtype]
        assert 0 <= off and off + sz <= LIMIT, (name, off, sz, LIMIT)
        return nc.alloc_sbuf_tensor_at(name, shape, dtype, offset=BASE + off)

    i128 = at("i128c", [128, 128], BF16, 0)
    nstr = at("nstrc", [128, 256], BF16, 256)
    mtri = at("mtric", [128, 128], BF16, 768)
    mneg = at("mnegc", [128, 128], BF16, 1664)
    segs = at("segsc", [128, 2], BF16, 1024)
    cvec = at("cvecc", [128, 16], F32, 1088)
    b1t = at("b1tc", [128, FC], F32, 1152)
    b2t = at("b2tc", [128, FC], F32, 1280)
    lnv = at("lnvc", [128, 64], F32, 1408)
    S1, S2, S3 = 2048, 67584, 133120
    assert 1664 + 256 <= S1

    for cv in [0.0, 1.0, -1.0, 1e-5, -0.5, 1.0 / 16.0, SCALE, -C_SOFT, 0.5,
               1.0 / HID, 2.0, -1.0 / HID, 0.25, 0.125, 8.0, 64.0]:
        pass
    CONSTS = [0.0, 1.0, -1.0, 1e-5, -0.5, 1.0 / 16.0, SCALE, -C_SOFT, 0.5,
              1.0 / HID, 2.0, -1.0 / HID, 0.25, 0.125, 8.0, 64.0]
    for i, cv in enumerate(CONSTS):
        nc.const_aps.register(F32, float(cv), cvec[:, i:i + 1])

    xnT = at("xnT", [128, DC * N], BF16, S1)
    Wo_sb = at("Wo_sb", [128, DC * HID], BF16, S1)
    h1T = at("h1T", [128, FC * ETOK], BF16, S1)
    sqs = at("sqs", [128, HID], F32, S2)
    xnb = [at(f"xnb{i}", [128, HID], BF16, S2 + 8192 + i * 4096) for i in range(2)]
    stg_kl = at("stg_kl", [128, 512], BF16, S2 + 16384)
    stg_vb = at("stg_vb", [128, 8 * 128], BF16, S2 + 17408)
    stg_va = at("stg_va", [128, 8 * 66], BF16, S2 + 19456)
    stg_fm = at("stg_fm", [128, 512], BF16, S2 + 20608)
    stg_fm2 = at("stg_fm2", [128, 512], BF16, S2 + 21632)
    ekt = at("ekt", [128, 512], F32, S2 + 22656)
    glt = at("glt", [128, 512], F32, S2 + 24704)
    tmp1 = at("tmp1", [128, 512], F32, S2 + 26752)
    tmp2 = at("tmp2", [128, 512], F32, S2 + 28800)
    srecb = at("srecb", [128, 8], F32, S2 + 30848)
    stg_t8 = at("stg_t8", [128, 1024], BF16, S2 + 30912)
    stg_k8 = at("stg_k8", [64, 1024], BF16, S2 + 32960)
    qTall = at("qTall", [128, 16 * N], BF16, S2)
    h2T = at("h2T", [128, FC * ETOK], BF16, S2)
    xin = [at(f"xin{i}", [128, HID], F32, S3 + i * 8192) for i in range(2)]
    Wk_sb = at("Wk_sb", [128, DC * 512], BF16, S3 + 16384)
    Wv_sb = at("Wv_sb", [128, DC * 512], BF16, S3 + 32768)
    Wq_sb = [at(f"Wq_sb{i}", [128, DC * 128], BF16, S3 + 49152 + i * 8192) for i in range(2)]
    gsl = []
    for i in range(2):
        b0 = S3 + i * 22016
        gsl.append(dict(
            kT=at(f"g{i}_kT", [128, N], BF16, b0),
            klT=at(f"g{i}_klT", [128, N], BF16, b0 + 4096),
            kbT=at(f"g{i}_kbT", [128, N], BF16, b0 + 8192),
            klt=at(f"g{i}_klt", [128, NT * DH], BF16, b0 + 12288),
            vbkb=at(f"g{i}_vbkb", [128, NT * 128], BF16, b0 + 14336),
            vaug=at(f"g{i}_vaug", [128, NT * 66], BF16, b0 + 18432),
        ))
    CT = S3 + 44032
    eqT = at("eqT", [128, 2 * N], BF16, CT)
    PT1 = [at(f"PT1_{i}", [128, 256], BF16, CT + 8192 + i * 512) for i in range(2)]
    PT2 = [at(f"PT2_{i}", [128, 256], BF16, CT + 9216 + i * 512) for i in range(2)]
    PT4 = [at(f"PT4_{i}", [128, 256], BF16, CT + 10240 + i * 512) for i in range(2)]
    T8s = [at(f"T8s_{i}", [128, 128], BF16, CT + 11264 + i * 256) for i in range(2)]
    Ybuf = [[at(f"Ybuf{i}_{j}", [128, 128], BF16, CT + 11776 + (i * 2 + j) * 256)
             for j in range(2)] for i in range(2)]
    u_p = [at(f"u_p{i}", [128, 64], BF16, CT + 12800 + i * 128) for i in range(2)]
    wTs = [at(f"wTs{i}", [64, 128], BF16, CT + 13056 + i * 256) for i in range(2)]
    S_f = at("S_f", [64, 64], F32, CT + 13568)
    S_b = at("S_b", [128, 64], BF16, CT + 13824)
    atn = [at(f"atn{i}", [128, 128], BF16, CT + 13952 + i * 256) for i in range(2)]
    srg = at("srg", [128, 4 * NT], F32, CT + 14464)
    PTb = [at(f"PTb{i}", [128, 1024], BF16, CT + 14720 + i * 2048) for i in range(2)]
    rcp = at("rcp", [128, 8], F32, CT + 18816)
    cmb = at("cmb", [128, 64], F32, CT + 18880)
    ocg = at("ocg", [128, NT * 256], BF16, CT + 19136)
    olg = at("olg", [128, NT * 256], BF16, CT + 27328)
    assert CT + 27328 + 8192 - S3 <= 79584, CT + 27328 + 8192 - S3
    oc_in = [at(f"oc_in{i}", [128, HID], BF16, S3 + i * 4096) for i in range(2)]
    ocT = at("ocT", [128, DC * 128], BF16, S3 + 8192)
    att_f = [at(f"att_f{i}", [128, HID], F32, S3 + 12288 + i * 8192) for i in range(2)]
    xrd = [at(f"xrd{i}", [128, HID], F32, S3 + 28672 + i * 8192) for i in range(2)]
    d_scr = at("d_scr", [128, HID], F32, S3 + 45056)
    xn3T = at("xn3T", [128, DC * ETOK], BF16, S3)
    w3s = [at(f"w3s{i}", [128, 16 * 512], BF16, S3 + i * 16384) for i in range(2)]
    w1s = [at(f"w1s{i}", [128, DC * 128], BF16, S3 + 32768 + i * 4096) for i in range(2)]
    exn = at("exn", [128, HID], BF16, S3 + 36864)
    w2s = [at(f"w2s{i}", [128, 16 * 128], BF16, S3 + 40960 + i * 4096) for i in range(4)]
    ex1 = at("ex1", [128, HID], F32, S3 + 49152)
    eo3 = at("eo3", [128, HID], F32, S3 + 57344)
    b3b = at("b3b", [128, HID], F32, S3 + 65536)
    o3st = [at(f"o3st{i}", [128, 512], F32, S3 + 73728 + i * 2048) for i in range(2)]
    assert S3 + 73728 + 4096 <= LIMIT

    OW3, OW1, OW2 = ("OW3",), ("OW1",), ("OW2",)

    psum_guards, sems, dsems, sem_guards = [], {}, {}, []

    def psum(name, shape):
        g = nc.psum_tensor(name, shape, F32)
        t = g.__enter__()
        psum_guards.append(g)
        return t

    pq = psum("pq", [128, 2048])
    p4 = psum("p4", [128, 512])
    p5 = psum("p5", [128, 512])
    p6 = psum("p6", [128, 512])
    p7 = psum("p7", [128, 512])
    PQ = ["PQ0", "PQ1", "PQ2", "PQ3"]
    B4, B5, B6, B7 = "B4", "B5", "B6", "B7"

    for e in Track.ENGS:
        g = nc.semaphore(f"sem_{e}")
        sems[e] = g.__enter__(); sem_guards.append(g)

    def T(fn, r=(), w=(), tag=""):
        return tr.op("t", fn, r, w, tag=tag)

    def V(fn, r=(), w=(), tag=""):
        return tr.op("v", fn, r, w, tag=tag)

    def S(fn, r=(), w=(), tag=""):
        return tr.op("s", fn, r, w, tag=tag)

    def G(fn, r=(), w=(), tag=""):
        return tr.op("g", fn, r, w, tag=tag)

    def YD(fn, r=(), w=(), st="ymisc", tag=""):
        return tr.op("y", fn, r, w, dma=st, tag=tag)

    def GD(fn, r=(), w=(), st="gmisc", tag=""):
        return tr.op("g", fn, r, w, dma=st, tag=tag)

    YD(lambda e: e.dma_start(out=i128[:], in_=i128_d), w=[i128], st="ycst", tag="c0")
    YD(lambda e: e.dma_start(out=nstr[:], in_=nstr_d), w=[nstr], st="ycst", tag="c1")
    YD(lambda e: e.dma_start(out=mtri[:], in_=mtri_d), w=[mtri], st="ycst", tag="c2")
    YD(lambda e: e.dma_start(out=mneg[:], in_=mneg_d), w=[mneg], st="ycst", tag="c2b")
    YD(lambda e: e.dma_start(out=segs[:], in_=segs_d), w=[segs], st="ycst", tag="c3")
    YD(lambda e: e.dma_start(out=cvec[:], in_=cvec_d), w=[cvec], st="ycst", tag="c4")
    YD(lambda e: e.dma_start(out=b1t[:], in_=b1_d), w=[b1t], st="ycst", tag="c5")
    YD(lambda e: e.dma_start(out=b2t[:], in_=b2_d), w=[b2t], st="ycst", tag="c6")

    def ln_tile(xsb, x_tok, dst, dst_tok, sq_scr, sq_toks, vslot, tag):
        s6 = lnv[:, vslot:vslot + 24]
        mv = lnv[:, vslot + 24:vslot + 26]
        rs = lnv[:, vslot + 26:vslot + 27]
        mr = lnv[:, vslot + 27:vslot + 28]
        k6 = ("lnv", vslot, 6)
        kmv = ("lnv", vslot, 0)
        krs = ("lnv", vslot, 1)
        kmr = ("lnv", vslot, 2)
        for j in range(4):
            V(lambda e, j=j: e.bn_stats(s6[:, j * 6:(j + 1) * 6],
                                        xsb[:, j * 512:(j + 1) * 512]),
              r=[x_tok], w=[k6], tag=f"{tag}bs{j}")
        V(lambda e: e.bn_aggr(mv, s6), r=[k6], w=[kmv], tag=f"{tag}ba")
        S(lambda e: e.activation(rs, mv[:, 1:2], AF.Ln, bias=1e-5),
          r=[kmv, cvec], w=[krs], tag=f"{tag}ln")
        S(lambda e: e.activation(rs, rs, AF.Exp, scale=-0.5),
          r=[krs, cvec], w=[krs], tag=f"{tag}ex")
        V(lambda e: e.tensor_tensor(mr, mv[:, 0:1], rs, ALU.mult),
          r=[kmv, krs], w=[kmr], tag=f"{tag}mr")
        V(lambda e: e.tensor_scalar(dst[:], xsb[:], rs, mr, ALU.mult, ALU.subtract),
          r=[x_tok, kmr, krs], w=[dst_tok], tag=f"{tag}xn")

    def transpose16(src, src_tok, dst_view, dst_tok_fn, tag):
        for g4 in range(4):
            bank, btok = (p6, B6) if (g4 % 2) == 0 else (p7, B7)
            for c4 in range(4):
                dc = g4 * 4 + c4
                T(lambda e, dc=dc, c4=c4, bank=bank: e.transpose(
                    bank[:, c4 * 64:(c4 + 1) * 64].bitcast(BF16),
                    src[:, dc * 128:(dc + 1) * 128], i128[:]),
                  r=[src_tok, i128], w=[btok], tag=f"{tag}tp{dc}")
            V(lambda e, g4=g4, bank=bank: e.tensor_copy(
                dst_view(g4), bank[:, 0:256].bitcast(BF16).rearrange(
                    "p (c n) -> p c n", c=4)),
              r=[btok], w=[dst_tok_fn(g4 * 4 + c) for c in range(4)], tag=f"{tag}ev{g4}")

    def phaseA(b):
        xnT_v = xnT[:].rearrange("p (c n) -> p c n", c=DC)
        for tt in range(NT):
            xi, xb = xin[tt % 2], xnb[tt % 2]
            xtok = (xi.name, "x")
            YD(lambda e, tt=tt, xi=xi: e.dma_start(
                out=xi[:], in_=x_in[b * N + tt * 128: b * N + (tt + 1) * 128, :]),
               w=[xtok], st="yx", tag=f"A{b}x{tt}")
            ln_tile(xi, xtok, xb, (xb.name, "n"), None, [], (tt % 2) * 32,
                    f"A{b}t{tt}")
            transpose16(xb, (xb.name, "n"),
                        lambda g4, tt=tt: xnT_v[:, g4 * 4:(g4 + 1) * 4, tt * 128:(tt + 1) * 128],
                        lambda dc, tt=tt: ("xnT", dc, tt), f"A{b}T{tt}")

    def phaseB(b):
        GD(lambda e: e.dma_start(out=Wk_sb[:].rearrange("p (c n) -> p c n", c=DC),
                                 in_=Wk_d.rearrange("(c p) n -> p c n", p=128)),
           w=[Wk_sb.name], st="gwkv", tag=f"B{b}wk")
        GD(lambda e: e.dma_start(out=Wv_sb[:].rearrange("p (c n) -> p c n", c=DC),
                                 in_=Wv_d.rearrange("(c p) n -> p c n", p=128)),
           w=[Wv_sb.name], st="gwkv", tag=f"B{b}wv")

        for tt in range(NT):
            psk = pq[:, (tt % 2) * 512:(tt % 2) * 512 + 512]
            ktk = PQ[tt % 2]
            psv = pq[:, 1024 + (tt % 2) * 512: 1024 + (tt % 2) * 512 + 512]
            vtk = PQ[2 + tt % 2]
            for dc in range(DC):
                T(lambda e, dc=dc, tt=tt, psk=psk: e.matmul(
                    psk, xnT[:, dc * N + tt * 128: dc * N + (tt + 1) * 128],
                    Wk_sb[:, dc * 512:(dc + 1) * 512],
                    start=(dc == 0), stop=(dc == DC - 1)),
                  r=[("xnT", dc, tt), Wk_sb.name], w=[ktk], tag=f"B{b}km{tt}_{dc}")
            for dc in range(DC):
                T(lambda e, dc=dc, tt=tt, psv=psv: e.matmul(
                    psv, xnT[:, dc * N + tt * 128: dc * N + (tt + 1) * 128],
                    Wv_sb[:, dc * 512:(dc + 1) * 512],
                    start=(dc == 0), stop=(dc == DC - 1)),
                  r=[("xnT", dc, tt), Wv_sb.name], w=[vtk], tag=f"B{b}vm{tt}_{dc}")
            S(lambda e, psk=psk: e.activation(tmp1[:], psk, AF.Abs),
              r=[ktk], w=[tmp1.name], tag=f"B{b}ab{tt}")
            S(lambda e: e.activation(tmp1[:], tmp1[:], AF.Exp, scale=-1.0),
              r=[tmp1.name, cvec], w=[tmp1.name], tag=f"B{b}ex{tt}")
            S(lambda e: e.activation(tmp1[:], tmp1[:], AF.Ln, bias=1.0),
              r=[tmp1.name, cvec], w=[tmp1.name], tag=f"B{b}lg{tt}")
            V(lambda e, psk=psk: e.tensor_scalar(tmp2[:], psk, 0.0, None, ALU.min),
              r=[ktk], w=[tmp2.name], tag=f"B{b}mi{tt}")
            V(lambda e: e.tensor_tensor(tmp2[:], tmp2[:], tmp1[:], ALU.subtract),
              r=[tmp1.name, tmp2.name], w=[tmp2.name], tag=f"B{b}su{tt}")
            V(lambda e: e.tensor_scalar(glt[:], tmp2[:], 1.0 / 16.0, None, ALU.mult),
              r=[tmp2.name], w=[glt.name], tag=f"B{b}gl{tt}")
            S(lambda e, psk=psk: e.activation(ekt[:], psk, AF.Exp),
              r=[ktk], w=[ekt.name], tag=f"B{b}ek{tt}")
            V(lambda e: e.tensor_reduce(srecb[:].unsqueeze(2),
                                        ekt[:].rearrange("p (h d) -> p h d", h=8),
                                        AX.X, ALU.add),
              r=[ekt.name], w=[srecb.name], tag=f"B{b}sr{tt}")
            V(lambda e: e.reciprocal(srecb[:], srecb[:]),
              r=[srecb.name], w=[srecb.name], tag=f"B{b}rc{tt}")
            for hv in range(HKV):
                V(lambda e, hv=hv: e.tensor_scalar(
                    stg_kl[:, hv * 64:(hv + 1) * 64], ekt[:, hv * 64:(hv + 1) * 64],
                    srecb[:, hv:hv + 1], None, ALU.mult),
                  r=[ekt.name, srecb.name], w=[("skl", hv)], tag=f"B{b}kl{tt}_{hv}")
                V(lambda e, hv=hv: e.tensor_tensor(
                    stg_vb[:, hv * 128 + 64: hv * 128 + 128],
                    stg_kl[:, hv * 64:(hv + 1) * 64],
                    glt[:, hv * 64:(hv + 1) * 64], ALU.mult),
                  r=[("skl", hv), glt.name], w=[("svb", hv, 1)], tag=f"B{b}kb{tt}_{hv}")
                V(lambda e, hv=hv, psv=psv: e.tensor_tensor(
                    stg_vb[:, hv * 128: hv * 128 + 64],
                    psv[:, hv * 64:(hv + 1) * 64],
                    glt[:, hv * 64:(hv + 1) * 64], ALU.mult),
                  r=[vtk, glt.name], w=[("svb", hv, 0)], tag=f"B{b}vb{tt}_{hv}")
                V(lambda e, hv=hv, psv=psv: e.tensor_copy(
                    stg_va[:, hv * 66: hv * 66 + 64], psv[:, hv * 64:(hv + 1) * 64]),
                  r=[vtk], w=[("sva", hv)], tag=f"B{b}va{tt}_{hv}")
            if tt == 0:
                for hv in range(HKV):
                    G(lambda e, hv=hv: e.memset(stg_va[:, hv * 66 + 64: hv * 66 + 66], 1.0),
                      w=[("sva1", hv)], tag=f"B{b}on{hv}")
            for hv in range(HKV if B_LEVEL >= 2 else 0):
                T(lambda e, hv=hv: e.transpose(
                    p4[0:64, hv * 64:(hv + 1) * 64].bitcast(BF16),
                    stg_kl[:, hv * 64:(hv + 1) * 64], i128[:]),
                  r=[("skl", hv), i128], w=[B4], tag=f"B{b}t1{tt}_{hv}")
                T(lambda e, hv=hv: e.transpose(
                    p4[64:128, hv * 64:(hv + 1) * 64].bitcast(BF16),
                    stg_vb[:, hv * 128 + 64: hv * 128 + 128], i128[:]),
                  r=[("svb", hv, 1), i128], w=[B4], tag=f"B{b}t2{tt}_{hv}")
            if B_LEVEL < 2:
                continue
            V(lambda e: e.tensor_copy(stg_t8[0:64, :], p4[0:64, :].bitcast(BF16)),
              r=[B4], w=[("st8", 0)], tag=f"B{b}e1{tt}")
            V(lambda e: e.tensor_copy(stg_t8[64:128, :], p4[0:64, :].bitcast(BF16)),
              r=[B4], w=[("st8", 1)], tag=f"B{b}e2{tt}")
            V(lambda e: e.tensor_copy(stg_k8[:], p4[64:128, :].bitcast(BF16)),
              r=[B4], w=[("sk8",)], tag=f"B{b}e3{tt}")
            YD(lambda e, tt=tt: e.dma_start(
                out=klT_s[:, :, tt * 128:(tt + 1) * 128].rearrange("h p n -> p h n"),
                in_=stg_t8[:].rearrange("p (h n) -> p h n", h=8)),
               r=[("st8", 0), ("st8", 1)], w=[("hklT", tt)], st="yklT",
               tag=f"B{b}d1{tt}")
            GD(lambda e, tt=tt: e.dma_start(
                out=kbT_s[:, :, tt * 128:(tt + 1) * 128].rearrange("h p n -> p h n"),
                in_=stg_k8[:].rearrange("p (h n) -> p h n", h=8)),
               r=[("sk8",)], w=[("hkbT", tt)], st="gkbT", tag=f"B{b}d2{tt}")
            if B_LEVEL < 3:
                continue
            YD(lambda e, tt=tt: e.dma_start(
                out=klt_s[:, tt * 128:(tt + 1) * 128, :].rearrange("k t n -> t k n"),
                in_=stg_kl[:].rearrange("p (k n) -> p k n", k=8)),
               r=[("skl", hv) for hv in range(HKV)], w=[("hklt", tt)], st="yklt",
               tag=f"B{b}d3{tt}")
            YD(lambda e, tt=tt: e.dma_start(
                out=vbkb_s[:, tt * 128:(tt + 1) * 128, :].rearrange("k t n -> t k n"),
                in_=stg_vb[:].rearrange("p (k n) -> p k n", k=8)),
               r=[("svb", hv, z) for hv in range(HKV) for z in (0, 1)],
               w=[("hvb", tt)], st="gvb", tag=f"B{b}d4{tt}")
            YD(lambda e, tt=tt: e.dma_start(
                out=vaug_s[:, tt * 128:(tt + 1) * 128, :].rearrange("k t n -> t k n"),
                in_=stg_va[:].rearrange("p (k n) -> p k n", k=8)),
               r=[("sva", hv) for hv in range(HKV)] +
                 [("sva1", hv) for hv in range(HKV)],
               w=[("hva", tt)], st="gva", tag=f"B{b}d5{tt}")

        for kp in range(HKV // 2 if B_LEVEL >= 4 else 0):
            for tb in range(4):
                psf, ftk = (p5, B5) if (tb % 2) == 0 else (p6, B6)
                for dc in range(DC):
                    T(lambda e, kp=kp, tb=tb, dc=dc, psf=psf: e.matmul(
                        psf[:],
                        Wk_sb[:].rearrange("p (c n) -> p c n", c=DC)[:, dc, kp * 128:(kp + 1) * 128],
                        xnT[:, dc * N + tb * 512: dc * N + (tb + 1) * 512],
                        start=(dc == 0), stop=(dc == DC - 1)),
                      r=[("xnT", dc, tb * 4 + q4) for q4 in range(4)] + [Wk_sb.name],
                      w=[ftk], tag=f"B{b}f{kp}_{tb}_{dc}")
                V(lambda e, psf=psf: e.tensor_copy(stg_fm[0:64, :], psf[0:64, :]),
                  r=[ftk], w=[("sfm", 0)], tag=f"B{b}g0{kp}_{tb}")
                V(lambda e, psf=psf: e.tensor_copy(stg_fm[64:128, :], psf[0:64, :]),
                  r=[ftk], w=[("sfm", 1)], tag=f"B{b}g1{kp}_{tb}")
                V(lambda e, psf=psf: e.tensor_copy(stg_fm2[0:64, :], psf[64:128, :]),
                  r=[ftk], w=[("sfm2", 0)], tag=f"B{b}g2{kp}_{tb}")
                V(lambda e, psf=psf: e.tensor_copy(stg_fm2[64:128, :], psf[64:128, :]),
                  r=[ftk], w=[("sfm2", 1)], tag=f"B{b}g3{kp}_{tb}")
                YD(lambda e, kp=kp, tb=tb: e.dma_start(
                    out=kT_s[kp * 2, :, tb * 512:(tb + 1) * 512], in_=stg_fm[:]),
                   r=[("sfm", 0), ("sfm", 1)], w=[("hkT", kp * 2, tb)], st="ykT",
                   tag=f"B{b}h0{kp}_{tb}")
                YD(lambda e, kp=kp, tb=tb: e.dma_start(
                    out=kT_s[kp * 2 + 1, :, tb * 512:(tb + 1) * 512], in_=stg_fm2[:]),
                   r=[("sfm2", 0), ("sfm2", 1)], w=[("hkT", kp * 2 + 1, tb)], st="ykT",
                   tag=f"B{b}h1{kp}_{tb}")

    def phaseBB(b):
        for pp in range(16):
            ws = Wq_sb[pp % 2]
            GD(lambda e, pp=pp, ws=ws: e.dma_start(
                out=ws[:].rearrange("p (c n) -> p c n", c=DC),
                in_=Wq_d[:, pp * 128:(pp + 1) * 128].rearrange("(c p) n -> p c n", p=128)),
               w=[ws.name], st="gwq", tag=f"Q{b}w{pp}")
            for tb in range(4):
                psq = pq[:, (tb % 2) * 512:(tb % 2) * 512 + 512]
                qtk = PQ[tb % 2]
                for dc in range(DC):
                    T(lambda e, tb=tb, dc=dc, ws=ws, psq=psq: e.matmul(
                        psq, ws[:, dc * 128:(dc + 1) * 128],
                        xnT[:, dc * N + tb * 512: dc * N + (tb + 1) * 512],
                        start=(dc == 0), stop=(dc == DC - 1)),
                      r=[("xnT", dc, tb * 4 + q4) for q4 in range(4)] + [ws.name],
                      w=[qtk], tag=f"Q{b}m{pp}_{tb}_{dc}")
                V(lambda e, pp=pp, tb=tb, psq=psq: e.tensor_copy(
                    qTall[:, pp * N + tb * 512: pp * N + (tb + 1) * 512], psq),
                  r=[qtk], w=[("qT", pp, tb)], tag=f"Q{b}e{pp}_{tb}")

    def phaseC(b, g):
        sl = gsl[g % 2]
        st = g % 2
        YD(lambda e, g=g, sl=sl: e.dma_start(out=sl["kT"][:], in_=kT_s[g]),
           r=[("hkT", g, tb) for tb in range(4)],
           w=[("gkT", st)], st="ckT", tag=f"C{b}{g}LkT")
        YD(lambda e, g=g, sl=sl: e.dma_start(out=sl["klT"][:], in_=klT_s[g]),
           r=[("hklT", tt) for tt in range(NT)],
           w=[("gklT", st)], st="cklT", tag=f"C{b}{g}LklT")
        YD(lambda e, g=g, sl=sl: e.dma_start(out=sl["kbT"][64:128, :], in_=kbT_s[g]),
           r=[("hkbT", tt) for tt in range(NT)],
           w=[("gkbT", st)], st="gckbT", tag=f"C{b}{g}LkbT")
        YD(lambda e, g=g, sl=sl: e.dma_start(
            out=sl["klt"][:].rearrange("p (t n) -> p t n", t=NT),
            in_=klt_s[g].rearrange("(t p) n -> p t n", p=128)),
           r=[("hklt", tt) for tt in range(NT)],
           w=[("gklt", st)], st="cklt", tag=f"C{b}{g}Lklt")
        YD(lambda e, g=g, sl=sl: e.dma_start(
            out=sl["vbkb"][:].rearrange("p (t n) -> p t n", t=NT),
            in_=vbkb_s[g].rearrange("(t p) n -> p t n", p=128)),
           r=[("hvb", tt) for tt in range(NT)],
           w=[("gvbkb", st)], st="gcvb", tag=f"C{b}{g}Lvbkb")
        YD(lambda e, g=g, sl=sl: e.dma_start(
            out=sl["vaug"][:].rearrange("p (t n) -> p t n", t=NT),
            in_=vaug_s[g].rearrange("(t p) n -> p t n", p=128)),
           r=[("hva", tt) for tt in range(NT)],
           w=[("gvaug", st)], st="gcva", tag=f"C{b}{g}Lvaug")
        kTt, klTt, kbTt = ("gkT", st), ("gklT", st), ("gkbT", st)
        kltt, vbt, vat = ("gklt", st), ("gvbkb", st), ("gvaug", st)

        for pi in range(2):
            pp = 2 * g + pi
            S(lambda e, pi=pi, pp=pp: e.activation(
                eqT[:, pi * N:(pi + 1) * N], qTall[:, pp * N:(pp + 1) * N], AF.Exp),
              r=[("qT", pp, tb) for tb in range(4)], w=[("eqT", pi)], tag=f"C{b}{g}eq{pi}")
        for tt in range(NT):
            for pi in range(2):
                T(lambda e, pi=pi, tt=tt: e.matmul(
                    p7[:, pi * 2: pi * 2 + 2],
                    eqT[:, pi * N + tt * 128: pi * N + (tt + 1) * 128],
                    segs[:], start=True, stop=True),
                  r=[("eqT", pi), segs], w=[B7], tag=f"C{b}{g}sg{tt}_{pi}")
            V(lambda e, tt=tt: e.reciprocal(srg[:, tt * 4:(tt + 1) * 4], p7[:, 0:4]),
              r=[B7], w=[("srg", tt)], tag=f"C{b}{g}sr{tt}")

        G(lambda e: e.memset(S_f[:], 0.0), w=[S_f.name], tag=f"C{b}{g}z0")
        G(lambda e: e.memset(S_b[:], 0.0), w=[S_b.name], tag=f"C{b}{g}z1")

        def machinery(i):
            co = i * 128
            pt1, pt2, pt4, t8 = PT1[i % 2], PT2[i % 2], PT4[i % 2], T8s[i % 2]
            T(lambda e, co=co: e.matmul(
                pq[:, 0:256][:, 0:128], sl["kbT"][64:128, co:co + 128],
                sl["klT"][64:128, co:co + 128],
                start=True, stop=True, tile_position=(64, 0)),
              r=[kbTt, klTt], w=[PQ[0]], tag=f"C{b}{g}A{i}")
            T(lambda e, co=co: e.matmul(
                pq[:, 128:256], sl["klT"][64:128, co:co + 128],
                sl["kbT"][64:128, co:co + 128],
                start=True, stop=True, tile_position=(64, 0)),
              r=[kbTt, klTt], w=[PQ[0]], tag=f"C{b}{g}At{i}")
            V(lambda e, pt1=pt1: e.tensor_tensor(pt1[:], pq[:, 0:256], nstr[:], ALU.mult),
              r=[PQ[0], nstr], w=[pt1.name], tag=f"C{b}{g}P1{i}")
            T(lambda e, pt1=pt1: e.matmul(pq[:, 512:640], pt1[:, 128:256], pt1[:, 0:128],
                                          start=True, stop=True),
              r=[pt1.name], w=[PQ[1]], tag=f"C{b}{g}q2a{i}")
            T(lambda e, pt1=pt1: e.matmul(pq[:, 640:768], pt1[:, 0:128], pt1[:, 128:256],
                                          start=True, stop=True),
              r=[pt1.name], w=[PQ[1]], tag=f"C{b}{g}q2b{i}")
            V(lambda e, pt2=pt2: e.tensor_copy(pt2[:], pq[:, 512:768]),
              r=[PQ[1]], w=[pt2.name], tag=f"C{b}{g}P2{i}")
            T(lambda e, pt2=pt2: e.matmul(pq[:, 0:128], pt2[:, 128:256], pt2[:, 0:128],
                                          start=True, stop=True),
              r=[pt2.name], w=[PQ[0]], tag=f"C{b}{g}q4a{i}")
            T(lambda e, pt2=pt2: e.matmul(pq[:, 128:256], pt2[:, 0:128], pt2[:, 128:256],
                                          start=True, stop=True),
              r=[pt2.name], w=[PQ[0]], tag=f"C{b}{g}q4b{i}")
            V(lambda e, pt4=pt4: e.tensor_copy(pt4[:], pq[:, 0:256]),
              r=[PQ[0]], w=[pt4.name], tag=f"C{b}{g}P4{i}")
            T(lambda e, pt4=pt4: e.matmul(pq[:, 512:640], pt4[:, 0:128], pt4[:, 128:256],
                                          start=True, stop=True),
              r=[pt4.name], w=[PQ[1]], tag=f"C{b}{g}q8{i}")
            V(lambda e, t8=t8: e.tensor_copy(t8[:], pq[:, 512:640]),
              r=[PQ[1]], w=[t8.name], tag=f"C{b}{g}T8{i}")
            ycur_ap = (lambda co=co: sl["vbkb"][:, co:co + 128])
            ycur_tok = vbt
            for lv in range(4):
                Tk = (pt1, pt2, pt4, t8)[lv]
                Tk_ap = (lambda Tk=Tk, lv=lv: Tk[:, 128:256] if lv < 3 else Tk[:])
                psy = pq[:, 1024:1152] if (lv % 2) == 0 else pq[:, 1536:1664]
                ytk = PQ[2 + lv % 2]
                T(lambda e, Tk_ap=Tk_ap, ycur_ap=ycur_ap, psy=psy: e.matmul(
                    psy, Tk_ap(), ycur_ap(), start=True, stop=False),
                  r=[Tk.name, ycur_tok], w=[ytk], tag=f"C{b}{g}Y{i}_{lv}a")
                T(lambda e, ycur_ap=ycur_ap, psy=psy: e.matmul(
                    psy, i128[:], ycur_ap(), start=False, stop=True),
                  r=[i128, ycur_tok], w=[ytk], tag=f"C{b}{g}Y{i}_{lv}b")
                yb = Ybuf[i % 2][lv % 2]
                V(lambda e, psy=psy, yb=yb: e.tensor_copy(yb[:], psy),
                  r=[ytk], w=[yb.name], tag=f"C{b}{g}Ye{i}_{lv}")
                ycur_ap = (lambda yb=yb: yb[:])
                ycur_tok = yb.name
            y4 = Ybuf[i % 2][1]
            wt = wTs[i % 2]
            T(lambda e, y4=y4: e.transpose(pq[:, 768:832].bitcast(BF16)[0:64, :],
                                           y4[:, 64:128], i128[:]),
              r=[y4.name, i128], w=[PQ[1]], tag=f"C{b}{g}wt{i}")
            V(lambda e, wt=wt: e.tensor_copy(wt[:], pq[:, 768:832].bitcast(BF16)[0:64, :]),
              r=[PQ[1]], w=[wt.name], tag=f"C{b}{g}we{i}")

        def scan_q(i):
            co = i * 128
            y4 = Ybuf[i % 2][1]
            wt = wTs[i % 2]
            up = u_p[i % 2]
            T(lambda e, wt=wt: e.matmul(p4[:, 0:64], wt[:], S_b[0:64, :],
                                        start=True, stop=True),
              r=[wt.name, S_b.name], w=[B4], tag=f"C{b}{g}up{i}")
            V(lambda e, y4=y4, up=up: e.tensor_tensor(up[:], y4[:, 0:64], p4[:, 0:64],
                                                      ALU.subtract),
              r=[y4.name, B4], w=[up.name], tag=f"C{b}{g}ue{i}")
            for hh in range(4):
                pi, beta = hh // 2, (hh % 2) * 64
                aslot = atn[hh % 2]
                psa, patk = (p6, B6) if (hh % 2) == 0 else (p7, B7)
                T(lambda e, co=co, beta=beta, pi=pi, psa=psa: e.matmul(
                    psa[:, 128:256], sl["klT"][beta:beta + 64, co:co + 128],
                    eqT[beta:beta + 64, pi * N + co: pi * N + co + 128],
                    start=True, stop=True, tile_position=(beta, 0)),
                  r=[klTt, ("eqT", pi)], w=[patk], tag=f"C{b}{g}at{i}_{hh}")
                V(lambda e, psa=psa, aslot=aslot: e.tensor_tensor(
                    aslot[:], psa[:, 128:256], mtri[:], ALU.mult),
                  r=[patk, mtri], w=[aslot.name], tag=f"C{b}{g}ae{i}_{hh}")
                T(lambda e, co=co, beta=beta, pi=pi: e.matmul(
                    p5[:, 128:192], eqT[beta:beta + 64, pi * N + co: pi * N + co + 128],
                    S_b[beta:beta + 64, :], start=True, stop=False,
                    tile_position=(beta, 0)),
                  r=[("eqT", pi), S_b.name], w=[B5], tag=f"C{b}{g}o1{i}_{hh}")
                T(lambda e, aslot=aslot, up=up: e.matmul(
                    p5[:, 128:192], aslot[:], up[:], start=False, stop=True),
                  r=[aslot.name, up.name], w=[B5], tag=f"C{b}{g}o2{i}_{hh}")
                V(lambda e, i=i, hh=hh: e.tensor_scalar(
                    olg[:, i * 256 + hh * 64: i * 256 + (hh + 1) * 64], p5[:, 128:192],
                    srg[:, i * 4 + hh: i * 4 + hh + 1], 0.5, ALU.mult, ALU.mult),
                  r=[B5, ("srg", i)], w=[("olg", i, hh)], tag=f"C{b}{g}oe{i}_{hh}")
            T(lambda e, i=i, up=up: e.matmul(
                p4[0:64, 64:128], sl["klt"][:, i * 64:(i + 1) * 64], up[:],
                start=True, stop=True),
              r=[kltt, up.name], w=[B4], tag=f"C{b}{g}sd{i}")
            V(lambda e: e.tensor_tensor(S_f[:], S_f[:], p4[0:64, 64:128], ALU.add),
              r=[S_f.name, B4], w=[S_f.name], tag=f"C{b}{g}sa{i}")
            V(lambda e: e.tensor_copy(S_b[0:64, :], S_f[:]),
              r=[S_f.name], w=[S_b.name], tag=f"C{b}{g}sc0{i}")
            V(lambda e: e.tensor_copy(S_b[64:128, :], S_f[:]),
              r=[S_f.name], w=[S_b.name], tag=f"C{b}{g}sc1{i}")

        machinery(0)
        for i in range(NCH):
            if i + 1 < NCH:
                machinery(i + 1)
            scan_q(i)

        for hh in range(4):
            pi, beta = hh // 2, (hh % 2) * 64
            pp = 2 * g + pi
            for qb in range(4):
                nkt = 4 * qb + 4

                def emit_scores(kt):
                    sb = pq[:, (kt % 4) * 512:(kt % 4) * 512 + 512]
                    sbtk = PQ[kt % 4]
                    T(lambda e, kt=kt, qb=qb, beta=beta, pp=pp, sb=sb: e.matmul(
                        sb, sl["kT"][beta:beta + 64, kt * 128:(kt + 1) * 128],
                        qTall[beta:beta + 64, pp * N + qb * 512: pp * N + (qb + 1) * 512],
                        start=True, stop=True, tile_position=(beta, 0)),
                      r=[kTt, ("qT", pp, qb)], w=[sbtk], tag=f"C{b}{g}s{hh}_{qb}_{kt}")

                def emit_exp(pr):
                    pt = PTb[pr % 2]
                    half = (pr % 2) * 1024
                    S(lambda e, pt=pt, half=half: e.activation(
                        pt[:], pq[:, half:half + 1024], AF.Exp, bias=-C_SOFT, scale=SCALE),
                      r=[PQ[2 * (pr % 2)], PQ[2 * (pr % 2) + 1], cvec],
                      w=[pt.name], tag=f"C{b}{g}x{hh}_{qb}_{pr}")
                    for kt in (2 * pr, 2 * pr + 1):
                        if kt >= 4 * qb and kt < nkt:
                            sub = kt - 4 * qb
                            G(lambda e, pt=pt, kt=kt, sub=sub: e.tensor_tensor(
                                pt[:, (kt % 2) * 512 + sub * 128:(kt % 2) * 512 + (sub + 1) * 128],
                                pt[:, (kt % 2) * 512 + sub * 128:(kt % 2) * 512 + (sub + 1) * 128],
                                mtri[:], ALU.mult),
                              r=[pt.name, mtri], w=[pt.name], tag=f"C{b}{g}m{hh}_{qb}_{kt}")

                def emit_pv(pr):
                    pt = PTb[pr % 2]
                    for kt in (2 * pr, 2 * pr + 1):
                        if kt >= nkt:
                            continue
                        for sub in range(4):
                            if kt > 4 * qb + sub:
                                continue
                            pvb = (p4, p5, p6, p7)[sub]
                            pvt = (B4, B5, B6, B7)[sub]
                            T(lambda e, pt=pt, sub=sub, kt=kt, qb=qb, pvb=pvb: e.matmul(
                                pvb[:, 0:65],
                                pt[:, (kt % 2) * 512 + sub * 128:(kt % 2) * 512 + (sub + 1) * 128],
                                sl["vaug"][:, kt * 66: kt * 66 + 65],
                                start=(kt == 0), stop=(kt == 4 * qb + sub)),
                              r=[pt.name, vat], w=[pvt], tag=f"C{b}{g}p{hh}_{qb}_{kt}_{sub}")

                npr = nkt // 2
                emit_scores(0); emit_scores(1); emit_exp(0)
                for pr in range(1, npr):
                    emit_scores(2 * pr); emit_scores(2 * pr + 1)
                    emit_exp(pr)
                    emit_pv(pr - 1)
                emit_pv(npr - 1)
                for sub in range(4):
                    qt = qb * 4 + sub
                    pvb = (p4, p5, p6, p7)[sub]
                    pvt = (B4, B5, B6, B7)[sub]
                    V(lambda e, sub=sub, pvb=pvb: e.reciprocal(
                        rcp[:, sub:sub + 1], pvb[:, 64:65]),
                      r=[pvt], w=[("rcp", sub)], tag=f"C{b}{g}r{hh}_{qb}_{sub}")
                    V(lambda e, sub=sub, pvb=pvb: e.tensor_scalar(
                        cmb[:], pvb[:, 0:64],
                        rcp[:, sub:sub + 1], 0.5, ALU.mult, ALU.mult),
                      r=[pvt, ("rcp", sub)], w=[cmb.name], tag=f"C{b}{g}c{hh}_{qb}_{sub}")
                    V(lambda e, qt=qt, hh=hh: e.tensor_tensor(
                        ocg[:, qt * 256 + hh * 64: qt * 256 + (hh + 1) * 64],
                        cmb[:], olg[:, qt * 256 + hh * 64: qt * 256 + (hh + 1) * 64],
                        ALU.add),
                      r=[cmb.name, ("olg", qt, hh)], w=[("ocg", qt, hh)],
                      tag=f"C{b}{g}d{hh}_{qb}_{sub}")
        YD(lambda e, g=g: e.dma_start(
            out=ocomb_s[g].rearrange("(t p) n -> p t n", p=128),
            in_=ocg[:].rearrange("p (t n) -> p t n", t=NT)),
           r=[("ocg", qt, hh) for qt in range(NT) for hh in range(4)],
           w=[("hoc", g)], st="yoc", tag=f"C{b}{g}oc")

    def phaseD(b):
        for j in range(4):
            GD(lambda e, j=j: e.dma_start(
                out=Wo_sb[:].rearrange("p (c n) -> p c n", c=DC)[:, j * 4:(j + 1) * 4, :],
                in_=Wo_d[j * 512:(j + 1) * 512, :].rearrange("(c p) n -> p c n", p=128)),
               w=[("Wo", j)], st="gwo", tag=f"D{b}w{j}")
        wo_r = [("Wo", j) for j in range(4)]
        ocT_v = ocT[:].rearrange("p (c n) -> p c n", c=DC)
        for tt in range(NT):
            oc = oc_in[tt % 2]
            octok = (oc.name,)
            YD(lambda e, tt=tt, oc=oc: e.dma_start(
                out=oc[:].rearrange("p (g n) -> p g n", g=HKV),
                in_=ocomb_s[:, tt * 128:(tt + 1) * 128, :].rearrange("g t n -> t g n")),
               r=[("hoc", g2) for g2 in range(HKV)],
               w=[octok], st="yoc", tag=f"D{b}o{tt}")
            transpose16(oc, octok,
                        lambda g4: ocT_v[:, g4 * 4:(g4 + 1) * 4, :],
                        lambda dc: ("ocT", dc), f"D{b}T{tt}")
            af = att_f[tt % 2]
            aftok = (af.name,)
            for ob in range(4):
                psd = pq[:, (ob % 2) * 512:(ob % 2) * 512 + 512]
                dtk = PQ[ob % 2]
                for dc in range(DC):
                    T(lambda e, dc=dc, ob=ob, psd=psd: e.matmul(
                        psd, ocT[:, dc * 128:(dc + 1) * 128],
                        Wo_sb[:, dc * HID + ob * 512: dc * HID + (ob + 1) * 512],
                        start=(dc == 0), stop=(dc == DC - 1)),
                      r=[("ocT", dc)] + wo_r, w=[dtk], tag=f"D{b}m{tt}_{ob}_{dc}")
                V(lambda e, ob=ob, psd=psd, af=af: e.tensor_copy(
                    af[:, ob * 512:(ob + 1) * 512], psd),
                  r=[dtk], w=[aftok], tag=f"D{b}e{tt}_{ob}")
            xr = xrd[tt % 2]
            xrtok = (xr.name,)
            YD(lambda e, tt=tt, xr=xr: e.dma_start(
                out=xr[:], in_=x_in[b * N + tt * 128: b * N + (tt + 1) * 128, :]),
               w=[xrtok], st="yx", tag=f"D{b}x{tt}")
            ln_tile(af, aftok, d_scr, (d_scr.name,), None, [],
                    (tt % 2) * 32, f"D{b}l{tt}")
            V(lambda e, xr=xr: e.tensor_tensor(xr[:], xr[:], d_scr[:], ALU.add),
              r=[xrtok, (d_scr.name,)], w=[xrtok], tag=f"D{b}a{tt}")
            YD(lambda e, tt=tt, xr=xr: e.dma_start(
                out=x1_s[b * N + tt * 128: b * N + (tt + 1) * 128, :], in_=xr[:]),
               r=[xrtok], w=[("x1", b * NT + tt)], st="yx1o", tag=f"D{b}s{tt}")

    def phaseE(p):
        t0 = p * ETOK
        if p == 0:
            YD(lambda e: e.dma_start(
                out=b3b[:], in_=b3_d.unsqueeze(0).partition_broadcast(128).squeeze(1)),
               w=[b3b.name], st="ycst", tag="Eb3")
        xn3_v = xn3T[:].rearrange("p (c n) -> p c n", c=DC)
        for t8 in range(ET8):
            xtok = (ex1.name,)
            YD(lambda e, t8=t8: e.dma_start(
                out=ex1[:], in_=x1_s[t0 + t8 * 128: t0 + (t8 + 1) * 128, :]),
               r=[("x1", (t0 + t8 * 128) // 128)], w=[xtok], st="yx1i", tag=f"E{p}x{t8}")
            ln_tile(ex1, xtok, exn, OW1, None, [], 0, f"E{p}l{t8}")
            transpose16(exn, OW1,
                        lambda g4, t8=t8: xn3_v[:, g4 * 4:(g4 + 1) * 4, t8 * 128:(t8 + 1) * 128],
                        lambda dc, t8=t8: ("x3", dc, t8, OW3), f"E{p}T{t8}")
        for hc in range(FC):
            ws = w1s[hc % 2]
            wtok = OW1 if (hc % 2 == 1) else (ws.name,)
            GD(lambda e, hc=hc, ws=ws: e.dma_start(
                out=ws[:].rearrange("p (c n) -> p c n", c=DC),
                in_=W1_d[:, hc * 128:(hc + 1) * 128].rearrange("(c p) n -> p c n", p=128)),
               w=[wtok], st="gw1", tag=f"E{p}w1{hc}")
            for tb in range(ETOK // 512):
                slot = (hc * 2 + tb) % 3
                ps1 = (pq[:, 0:512], pq[:, 512:1024], p4)[slot]
                ps1 = ps1 if slot == 2 else ps1
                ps1 = ps1[:, 0:512] if slot == 2 else ps1
                ptk = (PQ[0], PQ[1], B4)[slot]
                for dc in range(DC):
                    T(lambda e, tb=tb, dc=dc, ws=ws, ps1=ps1: e.matmul(
                        ps1, ws[:, dc * 128:(dc + 1) * 128],
                        xn3T[:, dc * ETOK + tb * 512: dc * ETOK + (tb + 1) * 512],
                        start=(dc == 0), stop=(dc == DC - 1)),
                      r=[("x3", dc, tb * 4 + z, OW3) for z in range(4)] + [wtok],
                      w=[ptk], tag=f"E{p}1m{hc}_{tb}_{dc}")
                S(lambda e, hc=hc, tb=tb, ps1=ps1: e.activation(
                    h1T[:, hc * ETOK + tb * 512: hc * ETOK + (tb + 1) * 512], ps1,
                    AF.Relu, bias=b1t[:, hc:hc + 1]),
                  r=[ptk, b1t], w=[("h1", hc, tb)], tag=f"E{p}1e{hc}_{tb}")
        for fc in range(FC):
            for hf in range(2):
                ws = w2s[(fc % 2) * 2 + hf]
                GD(lambda e, fc=fc, hf=hf, ws=ws: e.dma_start(
                    out=ws[:].rearrange("p (c n) -> p c n", c=16),
                    in_=W2_d[hf * 2048:(hf + 1) * 2048, fc * 128:(fc + 1) * 128]
                        .rearrange("(c p) n -> p c n", p=128)),
                   w=[("w2", fc % 2, hf)], st=("gw2a", "gw2b")[hf], tag=f"E{p}w2{fc}_{hf}")
            for tb in range(ETOK // 512):
                slot = (fc * 2 + tb) % 3
                ps2 = (pq[:, 1024:1536], pq[:, 1536:2048], p5)[slot]
                ps2 = ps2[:, 0:512] if slot == 2 else ps2
                ptk = (PQ[2], PQ[3], B5)[slot]
                for f1 in range(FC):
                    ws = w2s[(fc % 2) * 2 + f1 // 16]
                    T(lambda e, tb=tb, f1=f1, ws=ws, ps2=ps2: e.matmul(
                        ps2, ws[:, (f1 % 16) * 128:((f1 % 16) + 1) * 128],
                        h1T[:, f1 * ETOK + tb * 512: f1 * ETOK + (tb + 1) * 512],
                        start=(f1 == 0), stop=(f1 == FC - 1)),
                      r=[("h1", f1, tb), ("w2", fc % 2, f1 // 16)], w=[ptk],
                      tag=f"E{p}2m{fc}_{tb}_{f1}")
                S(lambda e, fc=fc, tb=tb, ps2=ps2: e.activation(
                    h2T[:, fc * ETOK + tb * 512: fc * ETOK + (tb + 1) * 512], ps2,
                    AF.Relu, bias=b2t[:, fc:fc + 1]),
                  r=[ptk, b2t], w=[("h2", fc, tb)], tag=f"E{p}2e{fc}_{tb}")
        for ob in range(4):
            for hf in range(2):
                ws = w3s[hf]
                for qh_ in range(2):
                    GD(lambda e, ob=ob, hf=hf, ws=ws, qh_=qh_: e.dma_start(
                        out=ws[:].rearrange("p (c n) -> p c n", c=16)[:, qh_ * 8:(qh_ + 1) * 8, :],
                        in_=W3_d[hf * 2048 + qh_ * 1024: hf * 2048 + (qh_ + 1) * 1024,
                                 ob * 512:(ob + 1) * 512]
                            .rearrange("(c p) n -> p c n", p=128)),
                       w=[OW3, ("w3", hf, qh_)], st=("gw3a", "gw3b")[qh_],
                       tag=f"E{p}w3{ob}_{hf}_{qh_}")
            for t8 in range(ET8):
                ps3 = (p6, p7)[t8 % 2][:, 0:512]
                ptk = (B6, B7)[t8 % 2]
                for fc in range(FC):
                    ws = w3s[fc // 16]
                    T(lambda e, t8=t8, fc=fc, ws=ws, ps3=ps3: e.matmul(
                        ps3, h2T[:, fc * ETOK + t8 * 128: fc * ETOK + (t8 + 1) * 128],
                        ws[:, (fc % 16) * 512:((fc % 16) + 1) * 512],
                        start=(fc == 0), stop=(fc == FC - 1)),
                      r=[("h2", fc, t8 // 4), OW3, ("w3", fc // 16, (fc % 16) // 8)],
                      w=[ptk], tag=f"E{p}3m{ob}_{t8}_{fc}")
                ost = o3st[t8 % 2]
                V(lambda e, ob=ob, ps3=ps3, ost=ost: e.tensor_tensor(
                    ost[:], ps3, b3b[:, ob * 512:(ob + 1) * 512], ALU.add),
                  r=[ptk, b3b.name], w=[(ost.name,)], tag=f"E{p}3e{ob}_{t8}")
                YD(lambda e, ob=ob, t8=t8, ost=ost: e.dma_start(
                    out=o3_s[t8 * 128:(t8 + 1) * 128, ob * 512:(ob + 1) * 512],
                    in_=ost[:]),
                   r=[(ost.name,)], w=[("o3", t8, ob)], st="yo3o", tag=f"E{p}3d{ob}_{t8}")
        for t8 in range(ET8):
            YD(lambda e, t8=t8: e.dma_start(
                out=eo3[:], in_=o3_s[t8 * 128:(t8 + 1) * 128, :]),
               r=[("o3", t8, ob2) for ob2 in range(4)], w=[(eo3.name,)], st="yo3i",
               tag=f"E{p}y{t8}")
            YD(lambda e, t8=t8: e.dma_start(
                out=ex1[:], in_=x1_s[t0 + t8 * 128: t0 + (t8 + 1) * 128, :]),
               r=[("x1", (t0 + t8 * 128) // 128)], w=[(ex1.name,)], st="yx1i",
               tag=f"E{p}z{t8}")
            ln_tile(eo3, (eo3.name,), eo3, (eo3.name,), None, [],
                    32, f"E{p}f{t8}")
            V(lambda e: e.tensor_tensor(ex1[:], ex1[:], eo3[:], ALU.add),
              r=[(ex1.name,), (eo3.name,)], w=[(ex1.name,)], tag=f"E{p}g{t8}")
            YD(lambda e, t8=t8: e.dma_start(
                out=out_d[t0 + t8 * 128: t0 + (t8 + 1) * 128, :], in_=ex1[:]),
               r=[(ex1.name,)], w=[], st="yout", tag=f"E{p}o{t8}")

    for b in range(nb):
        if "A" in phases:
            phaseA(b)
        if "B" in phases:
            phaseB(b)
        tr.barrier()
        if "Q" in phases:
            phaseBB(b)
        tr.barrier()
        if "C" in phases:
            for g in range(ngroups):
                phaseC(b, g)
        tr.barrier()
        if "D" in phases:
            phaseD(b)
        tr.barrier()
    if "E" in phases:
        for p in range(npass):
            phaseE(p)
        tr.barrier()
    tr.finalize()
    for sname in tr.streams:
        g = nc.semaphore(f"dma_{sname}")
        dsems[sname] = g.__enter__(); sem_guards.append(g)

    with nc.Block() as block:
        @block.tensor
        def _(t):
            tr.emit_engine("t", t, sems, dsems)

        @block.vector
        def _(v):
            tr.emit_engine("v", v, sems, dsems)

        @block.scalar
        def _(s):
            tr.emit_engine("s", s, sems, dsems)

        @block.gpsimd
        def _(g):
            tr.emit_engine("g", g, sems, dsems)

        @block.sync
        def _(y):
            tr.emit_engine("y", y, sems, dsems)
            for sname, cnt in tr.streams.items():
                y.wait_ge(dsems[sname], 16 * cnt)

    for g in reversed(sem_guards):
        g.__exit__(None, None, None)
    for g in reversed(psum_guards):
        g.__exit__(None, None, None)
    return nc


_PROG = None


def _consts():
    i128 = np.eye(128, dtype=ml_dtypes.bfloat16)
    strict = np.tril(np.ones((128, 128), np.float32), -1)
    nstr = np.concatenate([-strict, -strict.T], axis=1).astype(ml_dtypes.bfloat16)
    mtri = np.triu(np.ones((128, 128), np.float32)).astype(ml_dtypes.bfloat16)
    mneg = (np.triu(np.ones((128, 128), np.float32), 1) * -1e38).astype(ml_dtypes.bfloat16)
    segs = np.zeros((128, 2), np.float32)
    segs[0:64, 0] = 1.0
    segs[64:128, 1] = 1.0
    segs = segs.astype(ml_dtypes.bfloat16)
    cvec = np.zeros((128, 16), np.float32)
    CONSTS = [0.0, 1.0, -1.0, 1e-5, -0.5, 1.0 / 16.0, SCALE, -C_SOFT, 0.5,
              1.0 / HID, 2.0, -1.0 / HID, 0.25, 0.125, 8.0, 64.0]
    for i, cv in enumerate(CONSTS):
        cvec[:, i] = cv
    return i128, nstr, mtri, segs, cvec, mneg


def kernel(L0_attn_hidden_states, L0_attn_Wq, L0_attn_Wk, L0_attn_Wv, L0_attn_Wo,
           L0_ffn_W1, L0_ffn_b1, L0_ffn_W2, L0_ffn_b2, L0_ffn_W3, L0_ffn_b3):
    global _PROG
    x = np.ascontiguousarray(np.asarray(L0_attn_hidden_states, dtype=np.float32))
    Wq = np.ascontiguousarray(np.asarray(L0_attn_Wq, np.float32))
    Wk = np.ascontiguousarray(np.asarray(L0_attn_Wk, np.float32))
    Wv = np.ascontiguousarray(np.asarray(L0_attn_Wv, np.float32))
    Wo = np.ascontiguousarray(np.asarray(L0_attn_Wo, np.float32))
    W1T = np.ascontiguousarray(np.asarray(L0_ffn_W1, np.float32).T)
    W2T = np.ascontiguousarray(np.asarray(L0_ffn_W2, np.float32).T)
    W3T = np.ascontiguousarray(np.asarray(L0_ffn_W3, np.float32).T)
    b1 = np.ascontiguousarray(np.asarray(L0_ffn_b1, np.float32).reshape(FF // 128, 128).T)
    b2 = np.ascontiguousarray(np.asarray(L0_ffn_b2, np.float32).reshape(FF // 128, 128).T)
    b3 = np.ascontiguousarray(np.asarray(L0_ffn_b3, np.float32))
    i128, nstr, mtri, segs, cvec, mneg = _consts()

    if _PROG is None:
        _PROG = build_program()

    in_maps = []
    for c in range(NCORES):
        xs = np.ascontiguousarray(x[c * NB:(c + 1) * NB].reshape(NB * N, HID))
        in_maps.append(dict(
            x=xs, Wq=Wq, Wk=Wk, Wv=Wv, Wo=Wo, W1T=W1T, W2T=W2T, W3T=W3T,
            b1=b1, b2=b2, b3=b3, i128=i128, negstrict=nstr, masktriu=mtri,
            maskneg=mneg, segones=segs, constvec=cvec))
    res = run_bass_kernel_spmd(_PROG, in_maps, core_ids=list(range(NCORES)))
    out = np.empty((16, N, HID), np.float32)
    for c in range(NCORES):
        out[c * NB:(c + 1) * NB] = res.results[c]["out"].reshape(NB, N, HID)
    return out


# version 09
# speedup vs baseline: 1.0460x; 1.0460x over previous
import math
import numpy as np
import ml_dtypes
import concourse.bass as bass
import concourse.mybir as mybir
from concourse.bass_utils import run_bass_kernel_spmd
from concourse.alu_op_type import AluOpType as ALU

F32 = mybir.dt.float32
BF16 = mybir.dt.bfloat16
AF = mybir.ActivationFunctionType
AX = mybir.AxisListType

NCORES = 8
NB = 2
N = 2048
HID = 2048
NT = N // 128
DC = HID // 128
H, DH, HKV, GRP = 32, 64, 8, 4
NCH = N // 128
FF = 4096
FC = FF // 128
EPASS = 4
ETOK = (NB * N) // EPASS
ET8 = ETOK // 128
C_SOFT = 12.0
B_LEVEL = 4
SCALE = 1.0 / math.sqrt(DH)


class Op:
    __slots__ = ("eng", "fn", "idx", "inc_no", "waits", "drain", "dma", "tag")

    def __init__(self, eng, fn, idx, dma, tag):
        self.eng, self.fn, self.idx, self.dma, self.tag = eng, fn, idx, dma, tag
        self.inc_no = None
        self.waits = []
        self.drain = False


class Track:
    ENGS = ("t", "v", "s", "g", "y")
    PSUMTOK = {"PQ0", "PQ1", "PQ2", "PQ3", "B4", "B5", "B6", "B7"}

    def __init__(self):
        self.ops = {e: [] for e in self.ENGS}
        self.last_writer = {}
        self.readers = {}
        self.drain_cover = {e: -1 for e in self.ENGS}
        self.streams = {}
        self.last_dma = {}

    @staticmethod
    def _tok(t):
        if isinstance(t, (str, tuple, int)):
            return t
        return ("H", t.name)

    def op(self, eng, fn, reads=(), writes=(), dma=None, tag=""):
        reads = [self._tok(t) for t in reads]
        writes = [self._tok(t) for t in writes]
        o = Op(eng, fn, len(self.ops[eng]), dma, tag)
        if dma is not None:
            k = self.streams.get(dma, 0) + 1
            self.streams[dma] = k
            o.inc_no = 16 * k
            self.last_dma[dma] = o
        deps = {}

        def add(d):
            if d is o:
                return
            key = ("d", d.dma) if d.dma is not None else ("e", d.eng)
            cur = deps.get(key)
            if cur is None or d.idx > cur.idx:
                deps[key] = d

        for t in reads:
            w = self.last_writer.get(t)
            if w is not None:
                add(w)
            if t in self.PSUMTOK and eng in ("s", "v"):
                for rd in self.readers.get(t, ()):
                    if rd.eng in ("s", "v") and rd.eng != eng:
                        add(rd)
        for t in writes:
            w = self.last_writer.get(t)
            if w is not None:
                add(w)
            for rd in self.readers.get(t, ()):
                add(rd)
        need_drain = False
        for d in deps.values():
            if d.dma is not None:
                o.waits.append(d)
            elif d.eng == eng:
                if eng != "t" and d.idx > self.drain_cover[eng]:
                    need_drain = True
            else:
                o.waits.append(d)
        if need_drain:
            o.drain = True
            self.drain_cover[eng] = o.idx - 1
        self.ops[eng].append(o)
        for t in reads:
            self.readers.setdefault(t, []).append(o)
        for t in writes:
            self.last_writer[t] = o
            self.readers[t] = []
        return o

    def barrier(self):
        dmas = list(self.last_dma.values())
        for e in self.ENGS:
            o = self.op(e, (lambda eng: eng.drain()),
                        reads=[("BAR", e2) for e2 in self.ENGS if e2 != e],
                        writes=[("BAR", e)], tag=f"bar_{e}")
            for d in dmas:
                if d not in o.waits:
                    o.waits.append(d)

    def finalize(self):
        for e in self.ENGS:
            for o in self.ops[e]:
                for d in o.waits:
                    if d.dma is None and d.inc_no is None:
                        d.inc_no = -1
        self.inc_total = {}
        for e in self.ENGS:
            c = 0
            for o in self.ops[e]:
                if o.dma is None and o.inc_no is not None:
                    c += 1
                    o.inc_no = c
            self.inc_total[e] = c

    def emit_engine(self, e, eng, sems, dsems):
        waited = {}
        for o in self.ops[e]:
            if o.drain:
                eng.drain()
            for d in o.waits:
                if d.dma is not None:
                    key, sem, val = ("d", d.dma), dsems[d.dma], d.inc_no
                else:
                    key, sem, val = ("e", d.eng), sems[d.eng], d.inc_no
                assert val is not None and val > 0, (o.tag, d.tag)
                if waited.get(key, 0) < val:
                    eng.wait_ge(sem, val)
                    waited[key] = val
            if o.dma is not None and o.inc_no > 16:
                key = ("d", o.dma)
                if waited.get(key, 0) < o.inc_no - 16:
                    eng.wait_ge(dsems[o.dma], o.inc_no - 16)
                    waited[key] = o.inc_no - 16
            inst = o.fn(eng)
            if o.dma is not None:
                inst.then_inc(dsems[o.dma], 16)
            elif o.inc_no is not None:
                inst.then_inc(sems[e], 1)


def build_program(phases="ABQCDE", nb=NB, ngroups=HKV, npass=EPASS, dbg=False):
    nc = bass.Bass("TRN2", target_bir_lowering=False)
    tr = Track()

    x_in = nc.dram_tensor("x", [NB * N, HID], F32, kind="ExternalInput").ap()
    Wq_d = nc.dram_tensor("Wq", [HID, H * DH], F32, kind="ExternalInput").ap()
    Wk_d = nc.dram_tensor("Wk", [HID, HKV * DH], F32, kind="ExternalInput").ap()
    Wv_d = nc.dram_tensor("Wv", [HID, HKV * DH], F32, kind="ExternalInput").ap()
    Wo_d = nc.dram_tensor("Wo", [H * DH, HID], F32, kind="ExternalInput").ap()
    W1_d = nc.dram_tensor("W1T", [HID, FF], F32, kind="ExternalInput").ap()
    W2_d = nc.dram_tensor("W2T", [FF, FF], F32, kind="ExternalInput").ap()
    W3_d = nc.dram_tensor("W3T", [FF, HID], F32, kind="ExternalInput").ap()
    b1_d = nc.dram_tensor("b1", [128, FF // 128], F32, kind="ExternalInput").ap()
    b2_d = nc.dram_tensor("b2", [128, FF // 128], F32, kind="ExternalInput").ap()
    b3_d = nc.dram_tensor("b3", [HID], F32, kind="ExternalInput").ap()
    i128_d = nc.dram_tensor("i128", [128, 128], BF16, kind="ExternalInput").ap()
    nstr_d = nc.dram_tensor("negstrict", [128, 256], BF16, kind="ExternalInput").ap()
    mtri_d = nc.dram_tensor("masktriu", [128, 128], BF16, kind="ExternalInput").ap()
    mneg_d = nc.dram_tensor("maskneg", [128, 128], BF16, kind="ExternalInput").ap()
    segs_d = nc.dram_tensor("segones", [128, 2], BF16, kind="ExternalInput").ap()
    cvec_d = nc.dram_tensor("constvec", [128, 16], F32, kind="ExternalInput").ap()
    out_d = nc.dram_tensor("out", [NB * N, HID], F32, kind="ExternalOutput").ap()

    KIND = "ExternalOutput" if dbg else "Internal"
    kT_s = nc.dram_tensor("kT_s", [HKV, 128, N], BF16, kind=KIND).ap()
    klT_s = nc.dram_tensor("klT_s", [HKV, 128, N], BF16, kind=KIND).ap()
    kbT_s = nc.dram_tensor("kbT_s", [HKV, 64, N], BF16, kind=KIND).ap()
    klt_s = nc.dram_tensor("klt_s", [HKV, N, DH], BF16, kind=KIND).ap()
    vbkb_s = nc.dram_tensor("vbkb_s", [HKV, N, 128], BF16, kind=KIND).ap()
    vaug_s = nc.dram_tensor("vaug_s", [HKV, N, 66], BF16, kind=KIND).ap()
    ocomb_s = nc.dram_tensor("ocomb_s", [HKV, N, 256], BF16, kind=KIND).ap()
    x1_s = nc.dram_tensor("x1_s", [NB * N, HID], F32, kind=KIND).ap()
    o3_s = nc.dram_tensor("o3_s", [ETOK, HID], BF16, kind=KIND).ap()
    dbgout = nc.dram_tensor("dbgout", [16 * N, HID], BF16, kind=KIND).ap() if dbg else None

    BASE = (nc.sbuf_base + 255) // 256 * 256
    LIMIT = nc.sbuf_top - BASE

    def at(name, shape, dtype, off):
        sz = shape[1] * {F32: 4, BF16: 2}[dtype]
        assert 0 <= off and off + sz <= LIMIT, (name, off, sz, LIMIT)
        return nc.alloc_sbuf_tensor_at(name, shape, dtype, offset=BASE + off)

    i128 = at("i128c", [128, 128], BF16, 0)
    nstr = at("nstrc", [128, 256], BF16, 256)
    mtri = at("mtric", [128, 128], BF16, 768)
    mneg = at("mnegc", [128, 128], BF16, 1792)
    segs = at("segsc", [128, 2], BF16, 1024)
    cvec = at("cvecc", [128, 16], F32, 1088)
    b1t = at("b1tc", [128, FC], F32, 1152)
    b2t = at("b2tc", [128, FC], F32, 1280)
    lnv = at("lnvc", [128, 96], F32, 1408)
    S1, S2, S3 = 2048, 67584, 133120
    assert 1664 + 256 <= S1

    for cv in [0.0, 1.0, -1.0, 1e-5, -0.5, 1.0 / 16.0, SCALE, -C_SOFT, 0.5,
               1.0 / HID, 2.0, -1.0 / HID, 0.25, 0.125, 8.0, 64.0]:
        pass
    CONSTS = [0.0, 1.0, -1.0, 1e-5, -0.5, 1.0 / 16.0, SCALE, -C_SOFT, 0.5,
              1.0 / HID, 2.0, -1.0 / HID, 0.25, 0.125, 8.0, 64.0]
    for i, cv in enumerate(CONSTS):
        nc.const_aps.register(F32, float(cv), cvec[:, i:i + 1])

    xnT = at("xnT", [128, DC * N], BF16, S1)
    Wo_sb = at("Wo_sb", [128, DC * HID], BF16, S1)
    h1T = at("h1T", [128, FC * ETOK], BF16, S1)
    sqs = at("sqs", [128, HID], F32, S2)
    xnb = [at(f"xnb{i}", [128, HID], BF16, S2 + 8192 + i * 4096) for i in range(2)]
    stg_kl = at("stg_kl", [128, 512], BF16, S2 + 16384)
    stg_vb = at("stg_vb", [128, 8 * 128], BF16, S2 + 17408)
    stg_va = at("stg_va", [128, 8 * 66], BF16, S2 + 19456)
    stg_fm = at("stg_fm", [128, 512], BF16, S2 + 20608)
    stg_fm2 = at("stg_fm2", [128, 512], BF16, S2 + 21632)
    ekt = at("ekt", [128, 512], F32, S2 + 22656)
    glt = at("glt", [128, 512], F32, S2 + 24704)
    tmp1 = at("tmp1", [128, 512], F32, S2 + 26752)
    tmp2 = at("tmp2", [128, 512], F32, S2 + 28800)
    srecb = at("srecb", [128, 8], F32, S2 + 30848)
    stg_t8 = at("stg_t8", [128, 1024], BF16, S2 + 30912)
    stg_k8 = at("stg_k8", [64, 1024], BF16, S2 + 32960)
    qTall = at("qTall", [128, 16 * N], BF16, S2)
    h2T = at("h2T", [128, FC * ETOK], BF16, S2)
    xin = [at(f"xin{i}", [128, HID], F32, S3 + i * 8192) for i in range(2)]
    Wk_sb = at("Wk_sb", [128, DC * 512], BF16, S3 + 16384)
    Wv_sb = at("Wv_sb", [128, DC * 512], BF16, S3 + 32768)
    Wq_sb = [at(f"Wq_sb{i}", [128, DC * 128], BF16, S3 + 49152 + i * 8192) for i in range(2)]
    gsl = []
    for i in range(2):
        b0 = S3 + i * 22016
        gsl.append(dict(
            kT=at(f"g{i}_kT", [128, N], BF16, b0),
            klT=at(f"g{i}_klT", [128, N], BF16, b0 + 4096),
            kbT=at(f"g{i}_kbT", [128, N], BF16, b0 + 8192),
            klt=at(f"g{i}_klt", [128, NT * DH], BF16, b0 + 12288),
            vbkb=at(f"g{i}_vbkb", [128, NT * 128], BF16, b0 + 14336),
            vaug=at(f"g{i}_vaug", [128, NT * 66], BF16, b0 + 18432),
        ))
    CT = S3 + 44032
    eqT = at("eqT", [128, 2 * N], BF16, CT)
    PT1 = [at(f"PT1_{i}", [128, 256], BF16, CT + 8192 + i * 512) for i in range(2)]
    PT2 = [at(f"PT2_{i}", [128, 256], BF16, CT + 9216 + i * 512) for i in range(2)]
    PT4 = [at(f"PT4_{i}", [128, 256], BF16, CT + 10240 + i * 512) for i in range(2)]
    T8s = [at(f"T8s_{i}", [128, 128], BF16, CT + 11264 + i * 256) for i in range(2)]
    Ybuf = [[at(f"Ybuf{i}_{j}", [128, 128], BF16, CT + 11776 + (i * 2 + j) * 256)
             for j in range(2)] for i in range(2)]
    u_p = [at(f"u_p{i}", [128, 64], BF16, CT + 12800 + i * 128) for i in range(2)]
    wTs = [at(f"wTs{i}", [64, 128], BF16, CT + 13056 + i * 256) for i in range(2)]
    S_f = at("S_f", [64, 64], F32, CT + 13568)
    S_b = at("S_b", [128, 64], BF16, CT + 13824)
    atn = [at(f"atn{i}", [128, 128], BF16, CT + 13952 + i * 256) for i in range(2)]
    srg = at("srg", [128, 4 * NT], F32, CT + 14464)
    PTb = [at(f"PTb{i}", [128, 1024], BF16, CT + 14720 + i * 2048) for i in range(2)]
    rcp = at("rcp", [128, 8], F32, CT + 18816)
    cmb = at("cmb", [128, 64], F32, CT + 18880)
    ocg = at("ocg", [128, NT * 256], BF16, CT + 19136)
    olg = at("olg", [128, NT * 256], BF16, CT + 27328)
    assert CT + 27328 + 8192 - S3 <= 79584, CT + 27328 + 8192 - S3
    oc_in = [at(f"oc_in{i}", [128, HID], BF16, S3 + i * 4096) for i in range(2)]
    ocT = at("ocT", [128, DC * 128], BF16, S3 + 8192)
    att_f = [at(f"att_f{i}", [128, HID], F32, S3 + 12288 + i * 8192) for i in range(2)]
    xrd = [at(f"xrd{i}", [128, HID], F32, S3 + 28672 + i * 8192) for i in range(2)]
    d_scr = at("d_scr", [128, HID], F32, S3 + 45056)
    xn3T = at("xn3T", [128, DC * ETOK], BF16, S3)
    w3s = [at(f"w3s{i}", [128, 16 * 512], BF16, S3 + i * 16384) for i in range(2)]
    w1s = [at(f"w1s{i}", [128, DC * 128], BF16, S3 + 32768 + i * 4096) for i in range(2)]
    exn = at("exn", [128, HID], BF16, S3 + 36864)
    w2s = [at(f"w2s{i}", [128, 16 * 128], BF16, S3 + 40960 + i * 4096) for i in range(4)]
    ex1 = [at(f"ex1_{i}", [128, HID], F32, (S3 + 57344, S2 + 49152)[i]) for i in range(2)]
    b3b = at("b3b", [128, HID], BF16, S3 + 65536)
    o3st = [at(f"o3st{i}", [128, 512], BF16, S3 + 69632 + i * 1024) for i in range(2)]
    o3b = [at(f"o3b{i}", [128, HID], BF16, S2 + 57344 + i * 4096) for i in range(2)]
    w3B = [at(f"w3B{i}", [128, 16 * 512], BF16, S1 + 32768 + i * 16384) for i in range(2)]
    assert S3 + 69632 + 2048 <= LIMIT

    OW3, OW1, OW2 = ("OW3",), ("OW1",), ("OW2",)

    psum_guards, sems, dsems, sem_guards = [], {}, {}, []

    def psum(name, shape):
        g = nc.psum_tensor(name, shape, F32)
        t = g.__enter__()
        psum_guards.append(g)
        return t

    pq = psum("pq", [128, 2048])
    p4 = psum("p4", [128, 512])
    p5 = psum("p5", [128, 512])
    p6 = psum("p6", [128, 512])
    p7 = psum("p7", [128, 512])
    PQ = ["PQ0", "PQ1", "PQ2", "PQ3"]
    B4, B5, B6, B7 = "B4", "B5", "B6", "B7"

    for e in Track.ENGS:
        g = nc.semaphore(f"sem_{e}")
        sems[e] = g.__enter__(); sem_guards.append(g)

    def T(fn, r=(), w=(), tag=""):
        return tr.op("t", fn, r, w, tag=tag)

    def V(fn, r=(), w=(), tag=""):
        return tr.op("v", fn, r, w, tag=tag)

    def S(fn, r=(), w=(), tag=""):
        return tr.op("s", fn, r, w, tag=tag)

    def G(fn, r=(), w=(), tag=""):
        return tr.op("g", fn, r, w, tag=tag)

    def YD(fn, r=(), w=(), st="ymisc", tag=""):
        return tr.op("y", fn, r, w, dma=st, tag=tag)

    def GD(fn, r=(), w=(), st="gmisc", tag=""):
        return tr.op("g", fn, r, w, dma=st, tag=tag)

    YD(lambda e: e.dma_start(out=i128[:], in_=i128_d), w=[i128], st="ycst", tag="c0")
    YD(lambda e: e.dma_start(out=nstr[:], in_=nstr_d), w=[nstr], st="ycst", tag="c1")
    YD(lambda e: e.dma_start(out=mtri[:], in_=mtri_d), w=[mtri], st="ycst", tag="c2")
    YD(lambda e: e.dma_start(out=mneg[:], in_=mneg_d), w=[mneg], st="ycst", tag="c2b")
    YD(lambda e: e.dma_start(out=segs[:], in_=segs_d), w=[segs], st="ycst", tag="c3")
    YD(lambda e: e.dma_start(out=cvec[:], in_=cvec_d), w=[cvec], st="ycst", tag="c4")
    YD(lambda e: e.dma_start(out=b1t[:], in_=b1_d), w=[b1t], st="ycst", tag="c5")
    YD(lambda e: e.dma_start(out=b2t[:], in_=b2_d), w=[b2t], st="ycst", tag="c6")

    def ln_tile(xsb, x_tok, dst, dst_tok, sq_scr, sq_toks, vslot, tag):
        s6 = lnv[:, vslot:vslot + 24]
        mv = lnv[:, vslot + 24:vslot + 26]
        rs = lnv[:, vslot + 26:vslot + 27]
        mr = lnv[:, vslot + 27:vslot + 28]
        k6 = ("lnv", vslot, 6)
        kmv = ("lnv", vslot, 0)
        krs = ("lnv", vslot, 1)
        kmr = ("lnv", vslot, 2)
        for j in range(4):
            V(lambda e, j=j: e.bn_stats(s6[:, j * 6:(j + 1) * 6],
                                        xsb[:, j * 512:(j + 1) * 512]),
              r=[x_tok], w=[k6], tag=f"{tag}bs{j}")
        V(lambda e: e.bn_aggr(mv, s6), r=[k6], w=[kmv], tag=f"{tag}ba")
        S(lambda e: e.activation(rs, mv[:, 1:2], AF.Ln, bias=1e-5),
          r=[kmv, cvec], w=[krs], tag=f"{tag}ln")
        S(lambda e: e.activation(rs, rs, AF.Exp, scale=-0.5),
          r=[krs, cvec], w=[krs], tag=f"{tag}ex")
        V(lambda e: e.tensor_tensor(mr, mv[:, 0:1], rs, ALU.mult),
          r=[kmv, krs], w=[kmr], tag=f"{tag}mr")
        V(lambda e: e.tensor_scalar(dst[:], xsb[:], rs, mr, ALU.mult, ALU.subtract),
          r=[x_tok, kmr, krs], w=[dst_tok], tag=f"{tag}xn")

    def transpose16(src, src_tok, dst_view, dst_tok_fn, tag):
        for g4 in range(4):
            bank, btok = (p6, B6) if (g4 % 2) == 0 else (p7, B7)
            for c4 in range(4):
                dc = g4 * 4 + c4
                T(lambda e, dc=dc, c4=c4, bank=bank: e.transpose(
                    bank[:, c4 * 64:(c4 + 1) * 64].bitcast(BF16),
                    src[:, dc * 128:(dc + 1) * 128], i128[:]),
                  r=[src_tok, i128], w=[btok], tag=f"{tag}tp{dc}")
            V(lambda e, g4=g4, bank=bank: e.tensor_copy(
                dst_view(g4), bank[:, 0:256].bitcast(BF16).rearrange(
                    "p (c n) -> p c n", c=4)),
              r=[btok], w=[dst_tok_fn(g4 * 4 + c) for c in range(4)], tag=f"{tag}ev{g4}")

    def phaseA(b):
        xnT_v = xnT[:].rearrange("p (c n) -> p c n", c=DC)
        for tt in range(NT):
            xi, xb = xin[tt % 2], xnb[tt % 2]
            xtok = (xi.name, "x")
            YD(lambda e, tt=tt, xi=xi: e.dma_start(
                out=xi[:], in_=x_in[b * N + tt * 128: b * N + (tt + 1) * 128, :]),
               w=[xtok], st="yx", tag=f"A{b}x{tt}")
            ln_tile(xi, xtok, xb, (xb.name, "n"), None, [], (tt % 2) * 32,
                    f"A{b}t{tt}")
            transpose16(xb, (xb.name, "n"),
                        lambda g4, tt=tt: xnT_v[:, g4 * 4:(g4 + 1) * 4, tt * 128:(tt + 1) * 128],
                        lambda dc, tt=tt: ("xnT", dc, tt), f"A{b}T{tt}")

    def phaseB(b):
        GD(lambda e: e.dma_start(out=Wk_sb[:].rearrange("p (c n) -> p c n", c=DC),
                                 in_=Wk_d.rearrange("(c p) n -> p c n", p=128)),
           w=[Wk_sb.name], st="gwkv", tag=f"B{b}wk")
        GD(lambda e: e.dma_start(out=Wv_sb[:].rearrange("p (c n) -> p c n", c=DC),
                                 in_=Wv_d.rearrange("(c p) n -> p c n", p=128)),
           w=[Wv_sb.name], st="gwkv", tag=f"B{b}wv")

        for tt in range(NT):
            psk = pq[:, (tt % 2) * 512:(tt % 2) * 512 + 512]
            ktk = PQ[tt % 2]
            psv = pq[:, 1024 + (tt % 2) * 512: 1024 + (tt % 2) * 512 + 512]
            vtk = PQ[2 + tt % 2]
            for dc in range(DC):
                T(lambda e, dc=dc, tt=tt, psk=psk: e.matmul(
                    psk, xnT[:, dc * N + tt * 128: dc * N + (tt + 1) * 128],
                    Wk_sb[:, dc * 512:(dc + 1) * 512],
                    start=(dc == 0), stop=(dc == DC - 1)),
                  r=[("xnT", dc, tt), Wk_sb.name], w=[ktk], tag=f"B{b}km{tt}_{dc}")
            for dc in range(DC):
                T(lambda e, dc=dc, tt=tt, psv=psv: e.matmul(
                    psv, xnT[:, dc * N + tt * 128: dc * N + (tt + 1) * 128],
                    Wv_sb[:, dc * 512:(dc + 1) * 512],
                    start=(dc == 0), stop=(dc == DC - 1)),
                  r=[("xnT", dc, tt), Wv_sb.name], w=[vtk], tag=f"B{b}vm{tt}_{dc}")
            S(lambda e, psk=psk: e.activation(tmp1[:], psk, AF.Abs),
              r=[ktk], w=[tmp1.name], tag=f"B{b}ab{tt}")
            S(lambda e: e.activation(tmp1[:], tmp1[:], AF.Exp, scale=-1.0),
              r=[tmp1.name, cvec], w=[tmp1.name], tag=f"B{b}ex{tt}")
            S(lambda e: e.activation(tmp1[:], tmp1[:], AF.Ln, bias=1.0),
              r=[tmp1.name, cvec], w=[tmp1.name], tag=f"B{b}lg{tt}")
            V(lambda e, psk=psk: e.tensor_scalar(tmp2[:], psk, 0.0, None, ALU.min),
              r=[ktk], w=[tmp2.name], tag=f"B{b}mi{tt}")
            V(lambda e: e.tensor_tensor(tmp2[:], tmp2[:], tmp1[:], ALU.subtract),
              r=[tmp1.name, tmp2.name], w=[tmp2.name], tag=f"B{b}su{tt}")
            V(lambda e: e.tensor_scalar(glt[:], tmp2[:], 1.0 / 16.0, None, ALU.mult),
              r=[tmp2.name], w=[glt.name], tag=f"B{b}gl{tt}")
            S(lambda e, psk=psk: e.activation(ekt[:], psk, AF.Exp),
              r=[ktk], w=[ekt.name], tag=f"B{b}ek{tt}")
            V(lambda e: e.tensor_reduce(srecb[:].unsqueeze(2),
                                        ekt[:].rearrange("p (h d) -> p h d", h=8),
                                        AX.X, ALU.add),
              r=[ekt.name], w=[srecb.name], tag=f"B{b}sr{tt}")
            V(lambda e: e.reciprocal(srecb[:], srecb[:]),
              r=[srecb.name], w=[srecb.name], tag=f"B{b}rc{tt}")
            for hv in range(HKV):
                V(lambda e, hv=hv: e.tensor_scalar(
                    stg_kl[:, hv * 64:(hv + 1) * 64], ekt[:, hv * 64:(hv + 1) * 64],
                    srecb[:, hv:hv + 1], None, ALU.mult),
                  r=[ekt.name, srecb.name], w=[("skl", hv)], tag=f"B{b}kl{tt}_{hv}")
                V(lambda e, hv=hv: e.tensor_tensor(
                    stg_vb[:, hv * 128 + 64: hv * 128 + 128],
                    stg_kl[:, hv * 64:(hv + 1) * 64],
                    glt[:, hv * 64:(hv + 1) * 64], ALU.mult),
                  r=[("skl", hv), glt.name], w=[("svb", hv, 1)], tag=f"B{b}kb{tt}_{hv}")
                V(lambda e, hv=hv, psv=psv: e.tensor_tensor(
                    stg_vb[:, hv * 128: hv * 128 + 64],
                    psv[:, hv * 64:(hv + 1) * 64],
                    glt[:, hv * 64:(hv + 1) * 64], ALU.mult),
                  r=[vtk, glt.name], w=[("svb", hv, 0)], tag=f"B{b}vb{tt}_{hv}")
                V(lambda e, hv=hv, psv=psv: e.tensor_copy(
                    stg_va[:, hv * 66: hv * 66 + 64], psv[:, hv * 64:(hv + 1) * 64]),
                  r=[vtk], w=[("sva", hv)], tag=f"B{b}va{tt}_{hv}")
            if tt == 0:
                for hv in range(HKV):
                    G(lambda e, hv=hv: e.memset(stg_va[:, hv * 66 + 64: hv * 66 + 66], 1.0),
                      w=[("sva1", hv)], tag=f"B{b}on{hv}")
            for hv in range(HKV if B_LEVEL >= 2 else 0):
                T(lambda e, hv=hv: e.transpose(
                    p4[0:64, hv * 64:(hv + 1) * 64].bitcast(BF16),
                    stg_kl[:, hv * 64:(hv + 1) * 64], i128[:]),
                  r=[("skl", hv), i128], w=[B4], tag=f"B{b}t1{tt}_{hv}")
                T(lambda e, hv=hv: e.transpose(
                    p4[64:128, hv * 64:(hv + 1) * 64].bitcast(BF16),
                    stg_vb[:, hv * 128 + 64: hv * 128 + 128], i128[:]),
                  r=[("svb", hv, 1), i128], w=[B4], tag=f"B{b}t2{tt}_{hv}")
            if B_LEVEL < 2:
                continue
            V(lambda e: e.tensor_copy(stg_t8[0:64, :], p4[0:64, :].bitcast(BF16)),
              r=[B4], w=[("st8", 0)], tag=f"B{b}e1{tt}")
            V(lambda e: e.tensor_copy(stg_t8[64:128, :], p4[0:64, :].bitcast(BF16)),
              r=[B4], w=[("st8", 1)], tag=f"B{b}e2{tt}")
            V(lambda e: e.tensor_copy(stg_k8[:], p4[64:128, :].bitcast(BF16)),
              r=[B4], w=[("sk8",)], tag=f"B{b}e3{tt}")
            YD(lambda e, tt=tt: e.dma_start(
                out=klT_s[:, :, tt * 128:(tt + 1) * 128].rearrange("h p n -> p h n"),
                in_=stg_t8[:].rearrange("p (h n) -> p h n", h=8)),
               r=[("st8", 0), ("st8", 1)], w=[("hklT", tt)], st="yklT",
               tag=f"B{b}d1{tt}")
            GD(lambda e, tt=tt: e.dma_start(
                out=kbT_s[:, :, tt * 128:(tt + 1) * 128].rearrange("h p n -> p h n"),
                in_=stg_k8[:].rearrange("p (h n) -> p h n", h=8)),
               r=[("sk8",)], w=[("hkbT", tt)], st="gkbT", tag=f"B{b}d2{tt}")
            if B_LEVEL < 3:
                continue
            YD(lambda e, tt=tt: e.dma_start(
                out=klt_s[:, tt * 128:(tt + 1) * 128, :].rearrange("k t n -> t k n"),
                in_=stg_kl[:].rearrange("p (k n) -> p k n", k=8)),
               r=[("skl", hv) for hv in range(HKV)], w=[("hklt", tt)], st="yklt",
               tag=f"B{b}d3{tt}")
            YD(lambda e, tt=tt: e.dma_start(
                out=vbkb_s[:, tt * 128:(tt + 1) * 128, :].rearrange("k t n -> t k n"),
                in_=stg_vb[:].rearrange("p (k n) -> p k n", k=8)),
               r=[("svb", hv, z) for hv in range(HKV) for z in (0, 1)],
               w=[("hvb", tt)], st="gvb", tag=f"B{b}d4{tt}")
            YD(lambda e, tt=tt: e.dma_start(
                out=vaug_s[:, tt * 128:(tt + 1) * 128, :].rearrange("k t n -> t k n"),
                in_=stg_va[:].rearrange("p (k n) -> p k n", k=8)),
               r=[("sva", hv) for hv in range(HKV)] +
                 [("sva1", hv) for hv in range(HKV)],
               w=[("hva", tt)], st="gva", tag=f"B{b}d5{tt}")

        for kp in range(HKV // 2 if B_LEVEL >= 4 else 0):
            for tb in range(4):
                psf, ftk = (p5, B5) if (tb % 2) == 0 else (p6, B6)
                for dc in range(DC):
                    T(lambda e, kp=kp, tb=tb, dc=dc, psf=psf: e.matmul(
                        psf[:],
                        Wk_sb[:].rearrange("p (c n) -> p c n", c=DC)[:, dc, kp * 128:(kp + 1) * 128],
                        xnT[:, dc * N + tb * 512: dc * N + (tb + 1) * 512],
                        start=(dc == 0), stop=(dc == DC - 1)),
                      r=[("xnT", dc, tb * 4 + q4) for q4 in range(4)] + [Wk_sb.name],
                      w=[ftk], tag=f"B{b}f{kp}_{tb}_{dc}")
                V(lambda e, psf=psf: e.tensor_copy(stg_fm[0:64, :], psf[0:64, :]),
                  r=[ftk], w=[("sfm", 0)], tag=f"B{b}g0{kp}_{tb}")
                V(lambda e, psf=psf: e.tensor_copy(stg_fm[64:128, :], psf[0:64, :]),
                  r=[ftk], w=[("sfm", 1)], tag=f"B{b}g1{kp}_{tb}")
                V(lambda e, psf=psf: e.tensor_copy(stg_fm2[0:64, :], psf[64:128, :]),
                  r=[ftk], w=[("sfm2", 0)], tag=f"B{b}g2{kp}_{tb}")
                V(lambda e, psf=psf: e.tensor_copy(stg_fm2[64:128, :], psf[64:128, :]),
                  r=[ftk], w=[("sfm2", 1)], tag=f"B{b}g3{kp}_{tb}")
                YD(lambda e, kp=kp, tb=tb: e.dma_start(
                    out=kT_s[kp * 2, :, tb * 512:(tb + 1) * 512], in_=stg_fm[:]),
                   r=[("sfm", 0), ("sfm", 1)], w=[("hkT", kp * 2, tb)], st="ykT",
                   tag=f"B{b}h0{kp}_{tb}")
                YD(lambda e, kp=kp, tb=tb: e.dma_start(
                    out=kT_s[kp * 2 + 1, :, tb * 512:(tb + 1) * 512], in_=stg_fm2[:]),
                   r=[("sfm2", 0), ("sfm2", 1)], w=[("hkT", kp * 2 + 1, tb)], st="ykT",
                   tag=f"B{b}h1{kp}_{tb}")

    def phaseBB(b):
        for pp in range(16):
            ws = Wq_sb[pp % 2]
            GD(lambda e, pp=pp, ws=ws: e.dma_start(
                out=ws[:].rearrange("p (c n) -> p c n", c=DC),
                in_=Wq_d[:, pp * 128:(pp + 1) * 128].rearrange("(c p) n -> p c n", p=128)),
               w=[ws.name], st="gwq", tag=f"Q{b}w{pp}")
            for tb in range(4):
                psq = pq[:, (tb % 2) * 512:(tb % 2) * 512 + 512]
                qtk = PQ[tb % 2]
                for dc in range(DC):
                    T(lambda e, tb=tb, dc=dc, ws=ws, psq=psq: e.matmul(
                        psq, ws[:, dc * 128:(dc + 1) * 128],
                        xnT[:, dc * N + tb * 512: dc * N + (tb + 1) * 512],
                        start=(dc == 0), stop=(dc == DC - 1)),
                      r=[("xnT", dc, tb * 4 + q4) for q4 in range(4)] + [ws.name],
                      w=[qtk], tag=f"Q{b}m{pp}_{tb}_{dc}")
                V(lambda e, pp=pp, tb=tb, psq=psq: e.tensor_copy(
                    qTall[:, pp * N + tb * 512: pp * N + (tb + 1) * 512], psq),
                  r=[qtk], w=[("qT", pp, tb)], tag=f"Q{b}e{pp}_{tb}")

    def phaseC(b, g):
        sl = gsl[g % 2]
        st = g % 2
        YD(lambda e, g=g, sl=sl: e.dma_start(out=sl["kT"][:], in_=kT_s[g]),
           r=[("hkT", g, tb) for tb in range(4)],
           w=[("gkT", st)], st="ckT", tag=f"C{b}{g}LkT")
        YD(lambda e, g=g, sl=sl: e.dma_start(out=sl["klT"][:], in_=klT_s[g]),
           r=[("hklT", tt) for tt in range(NT)],
           w=[("gklT", st)], st="cklT", tag=f"C{b}{g}LklT")
        YD(lambda e, g=g, sl=sl: e.dma_start(out=sl["kbT"][64:128, :], in_=kbT_s[g]),
           r=[("hkbT", tt) for tt in range(NT)],
           w=[("gkbT", st)], st="gckbT", tag=f"C{b}{g}LkbT")
        YD(lambda e, g=g, sl=sl: e.dma_start(
            out=sl["klt"][:].rearrange("p (t n) -> p t n", t=NT),
            in_=klt_s[g].rearrange("(t p) n -> p t n", p=128)),
           r=[("hklt", tt) for tt in range(NT)],
           w=[("gklt", st)], st="cklt", tag=f"C{b}{g}Lklt")
        YD(lambda e, g=g, sl=sl: e.dma_start(
            out=sl["vbkb"][:].rearrange("p (t n) -> p t n", t=NT),
            in_=vbkb_s[g].rearrange("(t p) n -> p t n", p=128)),
           r=[("hvb", tt) for tt in range(NT)],
           w=[("gvbkb", st)], st="gcvb", tag=f"C{b}{g}Lvbkb")
        YD(lambda e, g=g, sl=sl: e.dma_start(
            out=sl["vaug"][:].rearrange("p (t n) -> p t n", t=NT),
            in_=vaug_s[g].rearrange("(t p) n -> p t n", p=128)),
           r=[("hva", tt) for tt in range(NT)],
           w=[("gvaug", st)], st="gcva", tag=f"C{b}{g}Lvaug")
        kTt, klTt, kbTt = ("gkT", st), ("gklT", st), ("gkbT", st)
        kltt, vbt, vat = ("gklt", st), ("gvbkb", st), ("gvaug", st)

        for pi in range(2):
            pp = 2 * g + pi
            S(lambda e, pi=pi, pp=pp: e.activation(
                eqT[:, pi * N:(pi + 1) * N], qTall[:, pp * N:(pp + 1) * N], AF.Exp),
              r=[("qT", pp, tb) for tb in range(4)], w=[("eqT", pi)], tag=f"C{b}{g}eq{pi}")
        for tt in range(NT):
            for pi in range(2):
                T(lambda e, pi=pi, tt=tt: e.matmul(
                    p7[:, pi * 2: pi * 2 + 2],
                    eqT[:, pi * N + tt * 128: pi * N + (tt + 1) * 128],
                    segs[:], start=True, stop=True),
                  r=[("eqT", pi), segs], w=[B7], tag=f"C{b}{g}sg{tt}_{pi}")
            V(lambda e, tt=tt: e.reciprocal(srg[:, tt * 4:(tt + 1) * 4], p7[:, 0:4]),
              r=[B7], w=[("srg", tt)], tag=f"C{b}{g}sr{tt}")

        G(lambda e: e.memset(S_f[:], 0.0), w=[S_f.name], tag=f"C{b}{g}z0")
        G(lambda e: e.memset(S_b[:], 0.0), w=[S_b.name], tag=f"C{b}{g}z1")

        def machinery(i):
            co = i * 128
            pt1, pt2, pt4, t8 = PT1[i % 2], PT2[i % 2], PT4[i % 2], T8s[i % 2]
            T(lambda e, co=co: e.matmul(
                pq[:, 0:256][:, 0:128], sl["kbT"][64:128, co:co + 128],
                sl["klT"][64:128, co:co + 128],
                start=True, stop=True, tile_position=(64, 0)),
              r=[kbTt, klTt], w=[PQ[0]], tag=f"C{b}{g}A{i}")
            T(lambda e, co=co: e.matmul(
                pq[:, 128:256], sl["klT"][64:128, co:co + 128],
                sl["kbT"][64:128, co:co + 128],
                start=True, stop=True, tile_position=(64, 0)),
              r=[kbTt, klTt], w=[PQ[0]], tag=f"C{b}{g}At{i}")
            V(lambda e, pt1=pt1: e.tensor_tensor(pt1[:], pq[:, 0:256], nstr[:], ALU.mult),
              r=[PQ[0], nstr], w=[pt1.name], tag=f"C{b}{g}P1{i}")
            T(lambda e, pt1=pt1: e.matmul(pq[:, 512:640], pt1[:, 128:256], pt1[:, 0:128],
                                          start=True, stop=True),
              r=[pt1.name], w=[PQ[1]], tag=f"C{b}{g}q2a{i}")
            T(lambda e, pt1=pt1: e.matmul(pq[:, 640:768], pt1[:, 0:128], pt1[:, 128:256],
                                          start=True, stop=True),
              r=[pt1.name], w=[PQ[1]], tag=f"C{b}{g}q2b{i}")
            V(lambda e, pt2=pt2: e.tensor_copy(pt2[:], pq[:, 512:768]),
              r=[PQ[1]], w=[pt2.name], tag=f"C{b}{g}P2{i}")
            T(lambda e, pt2=pt2: e.matmul(pq[:, 0:128], pt2[:, 128:256], pt2[:, 0:128],
                                          start=True, stop=True),
              r=[pt2.name], w=[PQ[0]], tag=f"C{b}{g}q4a{i}")
            T(lambda e, pt2=pt2: e.matmul(pq[:, 128:256], pt2[:, 0:128], pt2[:, 128:256],
                                          start=True, stop=True),
              r=[pt2.name], w=[PQ[0]], tag=f"C{b}{g}q4b{i}")
            V(lambda e, pt4=pt4: e.tensor_copy(pt4[:], pq[:, 0:256]),
              r=[PQ[0]], w=[pt4.name], tag=f"C{b}{g}P4{i}")
            T(lambda e, pt4=pt4: e.matmul(pq[:, 512:640], pt4[:, 0:128], pt4[:, 128:256],
                                          start=True, stop=True),
              r=[pt4.name], w=[PQ[1]], tag=f"C{b}{g}q8{i}")
            V(lambda e, t8=t8: e.tensor_copy(t8[:], pq[:, 512:640]),
              r=[PQ[1]], w=[t8.name], tag=f"C{b}{g}T8{i}")
            ycur_ap = (lambda co=co: sl["vbkb"][:, co:co + 128])
            ycur_tok = vbt
            for lv in range(4):
                Tk = (pt1, pt2, pt4, t8)[lv]
                Tk_ap = (lambda Tk=Tk, lv=lv: Tk[:, 128:256] if lv < 3 else Tk[:])
                psy = pq[:, 1024:1152] if (lv % 2) == 0 else pq[:, 1536:1664]
                ytk = PQ[2 + lv % 2]
                T(lambda e, Tk_ap=Tk_ap, ycur_ap=ycur_ap, psy=psy: e.matmul(
                    psy, Tk_ap(), ycur_ap(), start=True, stop=False),
                  r=[Tk.name, ycur_tok], w=[ytk], tag=f"C{b}{g}Y{i}_{lv}a")
                T(lambda e, ycur_ap=ycur_ap, psy=psy: e.matmul(
                    psy, i128[:], ycur_ap(), start=False, stop=True),
                  r=[i128, ycur_tok], w=[ytk], tag=f"C{b}{g}Y{i}_{lv}b")
                yb = Ybuf[i % 2][lv % 2]
                V(lambda e, psy=psy, yb=yb: e.tensor_copy(yb[:], psy),
                  r=[ytk], w=[yb.name], tag=f"C{b}{g}Ye{i}_{lv}")
                ycur_ap = (lambda yb=yb: yb[:])
                ycur_tok = yb.name
            y4 = Ybuf[i % 2][1]
            wt = wTs[i % 2]
            T(lambda e, y4=y4: e.transpose(pq[:, 768:832].bitcast(BF16)[0:64, :],
                                           y4[:, 64:128], i128[:]),
              r=[y4.name, i128], w=[PQ[1]], tag=f"C{b}{g}wt{i}")
            V(lambda e, wt=wt: e.tensor_copy(wt[:], pq[:, 768:832].bitcast(BF16)[0:64, :]),
              r=[PQ[1]], w=[wt.name], tag=f"C{b}{g}we{i}")

        def scan_q(i):
            co = i * 128
            y4 = Ybuf[i % 2][1]
            wt = wTs[i % 2]
            up = u_p[i % 2]
            T(lambda e, wt=wt: e.matmul(p4[:, 0:64], wt[:], S_b[0:64, :],
                                        start=True, stop=True),
              r=[wt.name, S_b.name], w=[B4], tag=f"C{b}{g}up{i}")
            V(lambda e, y4=y4, up=up: e.tensor_tensor(up[:], y4[:, 0:64], p4[:, 0:64],
                                                      ALU.subtract),
              r=[y4.name, B4], w=[up.name], tag=f"C{b}{g}ue{i}")
            for hh in range(4):
                pi, beta = hh // 2, (hh % 2) * 64
                aslot = atn[hh % 2]
                psa, patk = (p6, B6) if (hh % 2) == 0 else (p7, B7)
                T(lambda e, co=co, beta=beta, pi=pi, psa=psa: e.matmul(
                    psa[:, 128:256], sl["klT"][beta:beta + 64, co:co + 128],
                    eqT[beta:beta + 64, pi * N + co: pi * N + co + 128],
                    start=True, stop=True, tile_position=(beta, 0)),
                  r=[klTt, ("eqT", pi)], w=[patk], tag=f"C{b}{g}at{i}_{hh}")
                V(lambda e, psa=psa, aslot=aslot: e.tensor_tensor(
                    aslot[:], psa[:, 128:256], mtri[:], ALU.mult),
                  r=[patk, mtri], w=[aslot.name], tag=f"C{b}{g}ae{i}_{hh}")
                T(lambda e, co=co, beta=beta, pi=pi: e.matmul(
                    p5[:, 128:192], eqT[beta:beta + 64, pi * N + co: pi * N + co + 128],
                    S_b[beta:beta + 64, :], start=True, stop=False,
                    tile_position=(beta, 0)),
                  r=[("eqT", pi), S_b.name], w=[B5], tag=f"C{b}{g}o1{i}_{hh}")
                T(lambda e, aslot=aslot, up=up: e.matmul(
                    p5[:, 128:192], aslot[:], up[:], start=False, stop=True),
                  r=[aslot.name, up.name], w=[B5], tag=f"C{b}{g}o2{i}_{hh}")
                V(lambda e, i=i, hh=hh: e.tensor_scalar(
                    olg[:, i * 256 + hh * 64: i * 256 + (hh + 1) * 64], p5[:, 128:192],
                    srg[:, i * 4 + hh: i * 4 + hh + 1], 0.5, ALU.mult, ALU.mult),
                  r=[B5, ("srg", i)], w=[("olg", i, hh)], tag=f"C{b}{g}oe{i}_{hh}")
            T(lambda e, i=i, up=up: e.matmul(
                p4[0:64, 64:128], sl["klt"][:, i * 64:(i + 1) * 64], up[:],
                start=True, stop=True),
              r=[kltt, up.name], w=[B4], tag=f"C{b}{g}sd{i}")
            V(lambda e: e.tensor_tensor(S_f[:], S_f[:], p4[0:64, 64:128], ALU.add),
              r=[S_f.name, B4], w=[S_f.name], tag=f"C{b}{g}sa{i}")
            V(lambda e: e.tensor_copy(S_b[0:64, :], S_f[:]),
              r=[S_f.name], w=[S_b.name], tag=f"C{b}{g}sc0{i}")
            V(lambda e: e.tensor_copy(S_b[64:128, :], S_f[:]),
              r=[S_f.name], w=[S_b.name], tag=f"C{b}{g}sc1{i}")

        machinery(0)
        for i in range(NCH):
            if i + 1 < NCH:
                machinery(i + 1)
            scan_q(i)

        for hh in range(4):
            pi, beta = hh // 2, (hh % 2) * 64
            pp = 2 * g + pi
            for qb in range(4):
                nkt = 4 * qb + 4

                def emit_scores(kt):
                    sb = pq[:, (kt % 4) * 512:(kt % 4) * 512 + 512]
                    sbtk = PQ[kt % 4]
                    T(lambda e, kt=kt, qb=qb, beta=beta, pp=pp, sb=sb: e.matmul(
                        sb, sl["kT"][beta:beta + 64, kt * 128:(kt + 1) * 128],
                        qTall[beta:beta + 64, pp * N + qb * 512: pp * N + (qb + 1) * 512],
                        start=True, stop=True, tile_position=(beta, 0)),
                      r=[kTt, ("qT", pp, qb)], w=[sbtk], tag=f"C{b}{g}s{hh}_{qb}_{kt}")

                def emit_exp(pr):
                    pt = PTb[pr % 2]
                    half = (pr % 2) * 1024
                    S(lambda e, pt=pt, half=half: e.activation(
                        pt[:], pq[:, half:half + 1024], AF.Exp, bias=-C_SOFT, scale=SCALE),
                      r=[PQ[2 * (pr % 2)], PQ[2 * (pr % 2) + 1], cvec],
                      w=[pt.name], tag=f"C{b}{g}x{hh}_{qb}_{pr}")
                    for kt in (2 * pr, 2 * pr + 1):
                        if kt >= 4 * qb and kt < nkt:
                            sub = kt - 4 * qb
                            G(lambda e, pt=pt, kt=kt, sub=sub: e.tensor_tensor(
                                pt[:, (kt % 2) * 512 + sub * 128:(kt % 2) * 512 + (sub + 1) * 128],
                                pt[:, (kt % 2) * 512 + sub * 128:(kt % 2) * 512 + (sub + 1) * 128],
                                mtri[:], ALU.mult),
                              r=[pt.name, mtri], w=[pt.name], tag=f"C{b}{g}m{hh}_{qb}_{kt}")

                def emit_pv(pr):
                    pt = PTb[pr % 2]
                    for kt in (2 * pr, 2 * pr + 1):
                        if kt >= nkt:
                            continue
                        for sub in range(4):
                            if kt > 4 * qb + sub:
                                continue
                            pvb = (p4, p5, p6, p7)[sub]
                            pvt = (B4, B5, B6, B7)[sub]
                            T(lambda e, pt=pt, sub=sub, kt=kt, qb=qb, pvb=pvb: e.matmul(
                                pvb[:, 0:65],
                                pt[:, (kt % 2) * 512 + sub * 128:(kt % 2) * 512 + (sub + 1) * 128],
                                sl["vaug"][:, kt * 66: kt * 66 + 65],
                                start=(kt == 0), stop=(kt == 4 * qb + sub)),
                              r=[pt.name, vat], w=[pvt], tag=f"C{b}{g}p{hh}_{qb}_{kt}_{sub}")

                npr = nkt // 2
                emit_scores(0); emit_scores(1); emit_exp(0)
                for pr in range(1, npr):
                    emit_scores(2 * pr); emit_scores(2 * pr + 1)
                    emit_exp(pr)
                    emit_pv(pr - 1)
                emit_pv(npr - 1)
                for sub in range(4):
                    qt = qb * 4 + sub
                    pvb = (p4, p5, p6, p7)[sub]
                    pvt = (B4, B5, B6, B7)[sub]
                    V(lambda e, sub=sub, pvb=pvb: e.reciprocal(
                        rcp[:, sub:sub + 1], pvb[:, 64:65]),
                      r=[pvt], w=[("rcp", sub)], tag=f"C{b}{g}r{hh}_{qb}_{sub}")
                    V(lambda e, sub=sub, pvb=pvb: e.tensor_scalar(
                        cmb[:], pvb[:, 0:64],
                        rcp[:, sub:sub + 1], 0.5, ALU.mult, ALU.mult),
                      r=[pvt, ("rcp", sub)], w=[cmb.name], tag=f"C{b}{g}c{hh}_{qb}_{sub}")
                    V(lambda e, qt=qt, hh=hh: e.tensor_tensor(
                        ocg[:, qt * 256 + hh * 64: qt * 256 + (hh + 1) * 64],
                        cmb[:], olg[:, qt * 256 + hh * 64: qt * 256 + (hh + 1) * 64],
                        ALU.add),
                      r=[cmb.name, ("olg", qt, hh)], w=[("ocg", qt, hh)],
                      tag=f"C{b}{g}d{hh}_{qb}_{sub}")
        YD(lambda e, g=g: e.dma_start(
            out=ocomb_s[g].rearrange("(t p) n -> p t n", p=128),
            in_=ocg[:].rearrange("p (t n) -> p t n", t=NT)),
           r=[("ocg", qt, hh) for qt in range(NT) for hh in range(4)],
           w=[("hoc", g)], st="yoc", tag=f"C{b}{g}oc")

    def phaseD(b):
        for j in range(4):
            GD(lambda e, j=j: e.dma_start(
                out=Wo_sb[:].rearrange("p (c n) -> p c n", c=DC)[:, j * 4:(j + 1) * 4, :],
                in_=Wo_d[j * 512:(j + 1) * 512, :].rearrange("(c p) n -> p c n", p=128)),
               w=[("Wo", j)], st="gwo", tag=f"D{b}w{j}")
        wo_r = [("Wo", j) for j in range(4)]
        ocT_v = ocT[:].rearrange("p (c n) -> p c n", c=DC)
        for tt in range(NT):
            oc = oc_in[tt % 2]
            octok = (oc.name,)
            YD(lambda e, tt=tt, oc=oc: e.dma_start(
                out=oc[:].rearrange("p (g n) -> p g n", g=HKV),
                in_=ocomb_s[:, tt * 128:(tt + 1) * 128, :].rearrange("g t n -> t g n")),
               r=[("hoc", g2) for g2 in range(HKV)],
               w=[octok], st="yoc", tag=f"D{b}o{tt}")
            transpose16(oc, octok,
                        lambda g4: ocT_v[:, g4 * 4:(g4 + 1) * 4, :],
                        lambda dc: ("ocT", dc), f"D{b}T{tt}")
            af = att_f[tt % 2]
            aftok = (af.name,)
            for ob in range(4):
                psd = pq[:, (ob % 2) * 512:(ob % 2) * 512 + 512]
                dtk = PQ[ob % 2]
                for dc in range(DC):
                    T(lambda e, dc=dc, ob=ob, psd=psd: e.matmul(
                        psd, ocT[:, dc * 128:(dc + 1) * 128],
                        Wo_sb[:, dc * HID + ob * 512: dc * HID + (ob + 1) * 512],
                        start=(dc == 0), stop=(dc == DC - 1)),
                      r=[("ocT", dc)] + wo_r, w=[dtk], tag=f"D{b}m{tt}_{ob}_{dc}")
                V(lambda e, ob=ob, psd=psd, af=af: e.tensor_copy(
                    af[:, ob * 512:(ob + 1) * 512], psd),
                  r=[dtk], w=[aftok], tag=f"D{b}e{tt}_{ob}")
            xr = xrd[tt % 2]
            xrtok = (xr.name,)
            YD(lambda e, tt=tt, xr=xr: e.dma_start(
                out=xr[:], in_=x_in[b * N + tt * 128: b * N + (tt + 1) * 128, :]),
               w=[xrtok], st="yx", tag=f"D{b}x{tt}")
            ln_tile(af, aftok, d_scr, (d_scr.name,), None, [],
                    (tt % 2) * 32, f"D{b}l{tt}")
            V(lambda e, xr=xr: e.tensor_tensor(xr[:], xr[:], d_scr[:], ALU.add),
              r=[xrtok, (d_scr.name,)], w=[xrtok], tag=f"D{b}a{tt}")
            YD(lambda e, tt=tt, xr=xr: e.dma_start(
                out=x1_s[b * N + tt * 128: b * N + (tt + 1) * 128, :], in_=xr[:]),
               r=[xrtok], w=[("x1", b * NT + tt)], st="yx1o", tag=f"D{b}s{tt}")

    def phaseE(p):
        t0 = p * ETOK
        if p == 0:
            GD(lambda e: e.dma_start(
                out=b3b[:], in_=b3_d.unsqueeze(0).partition_broadcast(128).squeeze(1)),
               w=[b3b.name], st="gwkv", tag="Eb3")
        xn3_v = xn3T[:].rearrange("p (c n) -> p c n", c=DC)
        for t8 in range(ET8):
            xe = ex1[t8 % 2]
            xtok = (xe.name,)
            YD(lambda e, t8=t8, xe=xe: e.dma_start(
                out=xe[:], in_=x1_s[t0 + t8 * 128: t0 + (t8 + 1) * 128, :]),
               r=[("x1", (t0 + t8 * 128) // 128)],
               w=[xtok] + ([("OH2B",)] if t8 % 2 == 1 else []),
               st="yx1i", tag=f"E{p}x{t8}")
            ln_tile(xe, xtok, exn, OW1, None, [], (t8 % 2) * 32, f"E{p}l{t8}")
            transpose16(exn, OW1,
                        lambda g4, t8=t8: xn3_v[:, g4 * 4:(g4 + 1) * 4, t8 * 128:(t8 + 1) * 128],
                        lambda dc, t8=t8: ("x3", dc, t8, OW3), f"E{p}T{t8}")
        for hc in range(FC):
            ws = w1s[hc % 2]
            wtok = OW1 if (hc % 2 == 1) else (ws.name,)
            GD(lambda e, hc=hc, ws=ws: e.dma_start(
                out=ws[:].rearrange("p (c n) -> p c n", c=DC),
                in_=W1_d[:, hc * 128:(hc + 1) * 128].rearrange("(c p) n -> p c n", p=128)),
               w=[wtok], st="gw1", tag=f"E{p}w1{hc}")
            for tb in range(ETOK // 512):
                slot = (hc * 2 + tb) % 3
                ps1 = (pq[:, 0:512], pq[:, 512:1024], p4)[slot]
                ps1 = ps1 if slot == 2 else ps1
                ps1 = ps1[:, 0:512] if slot == 2 else ps1
                ptk = (PQ[0], PQ[1], B4)[slot]
                for dc in range(DC):
                    T(lambda e, tb=tb, dc=dc, ws=ws, ps1=ps1: e.matmul(
                        ps1, ws[:, dc * 128:(dc + 1) * 128],
                        xn3T[:, dc * ETOK + tb * 512: dc * ETOK + (tb + 1) * 512],
                        start=(dc == 0), stop=(dc == DC - 1)),
                      r=[("x3", dc, tb * 4 + z, OW3) for z in range(4)] + [wtok],
                      w=[ptk], tag=f"E{p}1m{hc}_{tb}_{dc}")
                S(lambda e, hc=hc, tb=tb, ps1=ps1: e.activation(
                    h1T[:, hc * ETOK + tb * 512: hc * ETOK + (tb + 1) * 512], ps1,
                    AF.Relu, bias=b1t[:, hc:hc + 1]),
                  r=[ptk, b1t],
                  w=[("h1", hc, tb)] + ([("OH1B",)] if hc >= 16 else []),
                  tag=f"E{p}1e{hc}_{tb}")
        for fc in range(FC):
            for hf in range(2):
                ws = w2s[(fc % 2) * 2 + hf]
                GD(lambda e, fc=fc, hf=hf, ws=ws: e.dma_start(
                    out=ws[:].rearrange("p (c n) -> p c n", c=16),
                    in_=W2_d[hf * 2048:(hf + 1) * 2048, fc * 128:(fc + 1) * 128]
                        .rearrange("(c p) n -> p c n", p=128)),
                   w=[("w2", fc % 2, hf)], st=("gw2a", "gw2b")[hf], tag=f"E{p}w2{fc}_{hf}")
            for tb in range(ETOK // 512):
                slot = (fc * 2 + tb) % 3
                ps2 = (pq[:, 1024:1536], pq[:, 1536:2048], p5)[slot]
                ps2 = ps2[:, 0:512] if slot == 2 else ps2
                ptk = (PQ[2], PQ[3], B5)[slot]
                for f1 in range(FC):
                    ws = w2s[(fc % 2) * 2 + f1 // 16]
                    T(lambda e, tb=tb, f1=f1, ws=ws, ps2=ps2: e.matmul(
                        ps2, ws[:, (f1 % 16) * 128:((f1 % 16) + 1) * 128],
                        h1T[:, f1 * ETOK + tb * 512: f1 * ETOK + (tb + 1) * 512],
                        start=(f1 == 0), stop=(f1 == FC - 1)),
                      r=[("h1", f1, tb), ("w2", fc % 2, f1 // 16)] +
                        ([("OH1B",)] if f1 >= 16 else []),
                      w=[ptk], tag=f"E{p}2m{fc}_{tb}_{f1}")
                S(lambda e, fc=fc, tb=tb, ps2=ps2: e.activation(
                    h2T[:, fc * ETOK + tb * 512: fc * ETOK + (tb + 1) * 512], ps2,
                    AF.Relu, bias=b2t[:, fc:fc + 1]),
                  r=[ptk, b2t],
                  w=[("h2", fc, tb)] + ([("OH2B",)] if fc >= 24 else []),
                  tag=f"E{p}2e{fc}_{tb}")
        for ob in range(4):
            wset = (w3s, w3B)[ob % 2]
            for hf in range(2):
                ws = wset[hf]
                for qh_ in range(2):
                    GD(lambda e, ob=ob, hf=hf, ws=ws, qh_=qh_: e.dma_start(
                        out=ws[:].rearrange("p (c n) -> p c n", c=16)[:, qh_ * 8:(qh_ + 1) * 8, :],
                        in_=W3_d[hf * 2048 + qh_ * 1024: hf * 2048 + (qh_ + 1) * 1024,
                                 ob * 512:(ob + 1) * 512]
                            .rearrange("(c p) n -> p c n", p=128)),
                       w=[("w3", ob % 2, hf, qh_)] +
                         ([OW3] if ob % 2 == 0 else [("OH1B",)]),
                       st=("gw3a", "gw3b")[qh_], tag=f"E{p}w3{ob}_{hf}_{qh_}")
            for t8 in range(ET8):
                ps3 = (p6, p7)[t8 % 2][:, 0:512]
                ptk = (B6, B7)[t8 % 2]
                for fc in range(FC):
                    ws = wset[fc // 16]
                    T(lambda e, t8=t8, fc=fc, ws=ws, ps3=ps3: e.matmul(
                        ps3, h2T[:, fc * ETOK + t8 * 128: fc * ETOK + (t8 + 1) * 128],
                        ws[:, (fc % 16) * 512:((fc % 16) + 1) * 512],
                        start=(fc == 0), stop=(fc == FC - 1)),
                      r=[("h2", fc, t8 // 4), ("w3", ob % 2, fc // 16, (fc % 16) // 8)] +
                        ([("OH2B",)] if fc >= 24 else []),
                      w=[ptk], tag=f"E{p}3m{ob}_{t8}_{fc}")
                ost = o3st[t8 % 2]
                V(lambda e, ob=ob, ps3=ps3, ost=ost: e.tensor_tensor(
                    ost[:], ps3, b3b[:, ob * 512:(ob + 1) * 512], ALU.add),
                  r=[ptk, b3b.name], w=[(ost.name,)], tag=f"E{p}3e{ob}_{t8}")
                YD(lambda e, ob=ob, t8=t8, ost=ost: e.dma_start(
                    out=o3_s[t8 * 128:(t8 + 1) * 128, ob * 512:(ob + 1) * 512],
                    in_=ost[:]),
                   r=[(ost.name,)], w=[("o3", t8, ob)], st="yo3o", tag=f"E{p}3d{ob}_{t8}")
        for t8 in range(ET8):
            ob3 = o3b[t8 % 2]
            xe = ex1[t8 % 2]
            YD(lambda e, t8=t8, ob3=ob3: e.dma_start(
                out=ob3[:], in_=o3_s[t8 * 128:(t8 + 1) * 128, :]),
               r=[("o3", t8, ob2) for ob2 in range(4)],
               w=[(ob3.name,), ("OH2B",)], st="yo3i", tag=f"E{p}y{t8}")
            YD(lambda e, t8=t8, xe=xe: e.dma_start(
                out=xe[:], in_=x1_s[t0 + t8 * 128: t0 + (t8 + 1) * 128, :]),
               r=[("x1", (t0 + t8 * 128) // 128)],
               w=[(xe.name,)] + ([("OH2B",)] if t8 % 2 == 1 else []),
               st="yx1i", tag=f"E{p}z{t8}")
            vslot = 64
            s6 = lnv[:, vslot:vslot + 24]
            mv = lnv[:, vslot + 24:vslot + 26]
            rs = lnv[:, vslot + 26:vslot + 27]
            mr = lnv[:, vslot + 27:vslot + 28]
            k6, kmv, krs, kmr = (("lnv", vslot, j) for j in range(4))
            for j in range(4):
                V(lambda e, j=j, ob3=ob3: e.bn_stats(s6[:, j * 6:(j + 1) * 6],
                                                     ob3[:, j * 512:(j + 1) * 512]),
                  r=[(ob3.name,)], w=[k6], tag=f"E{p}f{t8}bs{j}")
            V(lambda e: e.bn_aggr(mv, s6), r=[k6], w=[kmv], tag=f"E{p}f{t8}ba")
            S(lambda e: e.activation(rs, mv[:, 1:2], AF.Ln, bias=1e-5),
              r=[kmv, cvec], w=[krs], tag=f"E{p}f{t8}ln")
            S(lambda e: e.activation(rs, rs, AF.Exp, scale=-0.5),
              r=[krs, cvec], w=[krs], tag=f"E{p}f{t8}ex")
            V(lambda e: e.tensor_tensor(mr, mv[:, 0:1], rs, ALU.mult),
              r=[kmv, krs], w=[kmr], tag=f"E{p}f{t8}mr")
            V(lambda e, xe=xe: e.tensor_scalar(xe[:], xe[:], 1.0, mr, ALU.mult,
                                               ALU.subtract),
              r=[(xe.name,), kmr], w=[(xe.name,)], tag=f"E{p}g{t8}a")
            V(lambda e, xe=xe, ob3=ob3: e.scalar_tensor_tensor(
                xe[:], ob3[:], rs, xe[:], ALU.mult, ALU.add),
              r=[(xe.name,), (ob3.name,), krs], w=[(xe.name,)], tag=f"E{p}g{t8}b")
            YD(lambda e, t8=t8, xe=xe: e.dma_start(
                out=out_d[t0 + t8 * 128: t0 + (t8 + 1) * 128, :], in_=xe[:]),
               r=[(xe.name,)], w=[], st="yout", tag=f"E{p}o{t8}")

    for b in range(nb):
        if "A" in phases:
            phaseA(b)
        if "B" in phases:
            phaseB(b)
        tr.barrier()
        if "Q" in phases:
            phaseBB(b)
        tr.barrier()
        if "C" in phases:
            for g in range(ngroups):
                phaseC(b, g)
        tr.barrier()
        if "D" in phases:
            phaseD(b)
        tr.barrier()
    if "E" in phases:
        for p in range(npass):
            phaseE(p)
        tr.barrier()
    tr.finalize()
    for sname in tr.streams:
        g = nc.semaphore(f"dma_{sname}")
        dsems[sname] = g.__enter__(); sem_guards.append(g)

    with nc.Block() as block:
        @block.tensor
        def _(t):
            tr.emit_engine("t", t, sems, dsems)

        @block.vector
        def _(v):
            tr.emit_engine("v", v, sems, dsems)

        @block.scalar
        def _(s):
            tr.emit_engine("s", s, sems, dsems)

        @block.gpsimd
        def _(g):
            tr.emit_engine("g", g, sems, dsems)

        @block.sync
        def _(y):
            tr.emit_engine("y", y, sems, dsems)
            for sname, cnt in tr.streams.items():
                y.wait_ge(dsems[sname], 16 * cnt)

    for g in reversed(sem_guards):
        g.__exit__(None, None, None)
    for g in reversed(psum_guards):
        g.__exit__(None, None, None)
    return nc


_PROG = None


def _consts():
    i128 = np.eye(128, dtype=ml_dtypes.bfloat16)
    strict = np.tril(np.ones((128, 128), np.float32), -1)
    nstr = np.concatenate([-strict, -strict.T], axis=1).astype(ml_dtypes.bfloat16)
    mtri = np.triu(np.ones((128, 128), np.float32)).astype(ml_dtypes.bfloat16)
    mneg = (np.triu(np.ones((128, 128), np.float32), 1) * -1e38).astype(ml_dtypes.bfloat16)
    segs = np.zeros((128, 2), np.float32)
    segs[0:64, 0] = 1.0
    segs[64:128, 1] = 1.0
    segs = segs.astype(ml_dtypes.bfloat16)
    cvec = np.zeros((128, 16), np.float32)
    CONSTS = [0.0, 1.0, -1.0, 1e-5, -0.5, 1.0 / 16.0, SCALE, -C_SOFT, 0.5,
              1.0 / HID, 2.0, -1.0 / HID, 0.25, 0.125, 8.0, 64.0]
    for i, cv in enumerate(CONSTS):
        cvec[:, i] = cv
    return i128, nstr, mtri, segs, cvec, mneg


def kernel(L0_attn_hidden_states, L0_attn_Wq, L0_attn_Wk, L0_attn_Wv, L0_attn_Wo,
           L0_ffn_W1, L0_ffn_b1, L0_ffn_W2, L0_ffn_b2, L0_ffn_W3, L0_ffn_b3):
    global _PROG
    x = np.ascontiguousarray(np.asarray(L0_attn_hidden_states, dtype=np.float32))
    Wq = np.ascontiguousarray(np.asarray(L0_attn_Wq, np.float32))
    Wk = np.ascontiguousarray(np.asarray(L0_attn_Wk, np.float32))
    Wv = np.ascontiguousarray(np.asarray(L0_attn_Wv, np.float32))
    Wo = np.ascontiguousarray(np.asarray(L0_attn_Wo, np.float32))
    W1T = np.ascontiguousarray(np.asarray(L0_ffn_W1, np.float32).T)
    W2T = np.ascontiguousarray(np.asarray(L0_ffn_W2, np.float32).T)
    W3T = np.ascontiguousarray(np.asarray(L0_ffn_W3, np.float32).T)
    b1 = np.ascontiguousarray(np.asarray(L0_ffn_b1, np.float32).reshape(FF // 128, 128).T)
    b2 = np.ascontiguousarray(np.asarray(L0_ffn_b2, np.float32).reshape(FF // 128, 128).T)
    b3 = np.ascontiguousarray(np.asarray(L0_ffn_b3, np.float32))
    i128, nstr, mtri, segs, cvec, mneg = _consts()

    if _PROG is None:
        _PROG = build_program()

    in_maps = []
    for c in range(NCORES):
        xs = np.ascontiguousarray(x[c * NB:(c + 1) * NB].reshape(NB * N, HID))
        in_maps.append(dict(
            x=xs, Wq=Wq, Wk=Wk, Wv=Wv, Wo=Wo, W1T=W1T, W2T=W2T, W3T=W3T,
            b1=b1, b2=b2, b3=b3, i128=i128, negstrict=nstr, masktriu=mtri,
            maskneg=mneg, segones=segs, constvec=cvec))
    res = run_bass_kernel_spmd(_PROG, in_maps, core_ids=list(range(NCORES)))
    out = np.empty((16, N, HID), np.float32)
    for c in range(NCORES):
        out[c * NB:(c + 1) * NB] = res.results[c]["out"].reshape(NB, N, HID)
    return out
